# version 05
# baseline (speedup 1.0000x reference)
import contextlib
import numpy as np
import concourse.bass as bass
import concourse.mybir as mybir
from concourse.bass_utils import run_bass_kernel_spmd

import contextlib
import numpy as np
import concourse.bass as bass
import concourse.mybir as mybir

F32 = mybir.dt.float32
BF16 = mybir.dt.bfloat16
AF = mybir.ActivationFunctionType
ALU = mybir.AluOpType

P = 128
C = 1024
CT = 8
T = 8192
F = 4096
FT = 32
EPS = 1e-5
PIH = float(np.pi / 2)

CV_TS, CV_U, CV_LN1W, CV_LN1B, CV_LN2W, CV_LN2B, CV_BDT, CV_DP, CV_B2 = \
    0, 8, 16, 24, 32, 40, 48, 56, 64
CV_B1, CV_TH, CV_QWR, CV_QWI, CV_ALOG = 72, 104, 136, 168, 200
NCV = 328

DV_TS1, DV_NEGTS, DV_ONEMTS, DV_U = 0, 8, 16, 24
DV_STH, DV_CTH, DV_CQP, DV_ANEG = 32, 64, 96, 128
DV_B1S, DV_B1SP = 256, 288
NDV = 320


def tsl(i, n):
    return slice(i * n, (i + 1) * n)


class KB:
    def __init__(self, nc):
        self.nc = nc
        self.sem = {}
        self.selfn = {}

    def S(self, name):
        if name not in self.sem:
            self.sem[name] = self.nc.alloc_semaphore(name)
        return self.sem[name]

    def chain(self, tag, eng, ins):
        sem = self.S("self_" + tag)
        n = self.selfn.get(tag, 0) + 1
        self.selfn[tag] = n
        ins.then_inc(sem, 1)
        eng.wait_ge(sem, n)
        return ins

    def alloc_consts(self):
        nc = self.nc
        self.cv = nc.alloc_sbuf_tensor("cv", [P, NCV], F32)
        self.dv = nc.alloc_sbuf_tensor("dv", [P, NDV], F32)
        self.c1024 = nc.alloc_sbuf_tensor("c1024", [P, 1], F32)
        self.ep1 = nc.alloc_sbuf_tensor("ep1", [1, 1], F32)
        self.mh1 = nc.alloc_sbuf_tensor("mh1", [1, 1], F32)
        self.one1 = nc.alloc_sbuf_tensor("one1", [1, 1], F32)
        self.one_pp = nc.alloc_sbuf_tensor("one_pp", [P, 1], F32)
        self.c01_pp = nc.alloc_sbuf_tensor("c01_pp", [P, 1], F32)
        self.jbg = nc.alloc_sbuf_tensor("jbg", [P, 16], BF16)

    def emit_consts(self, cvec_ap):
        kbself = self
        nc = self.nc
        ces = contextlib.ExitStack()
        self.half_pp = ces.enter_context(nc.sbuf_tensor("half_pp", [P, 1], F32))
        self.sc32a = ces.enter_context(nc.sbuf_tensor("sc32a", [P, 32], F32))
        self.sc32b = ces.enter_context(nc.sbuf_tensor("sc32b", [P, 32], F32))
        self.sc128 = ces.enter_context(nc.sbuf_tensor("sc128", [P, 128], F32))
        self._ces = ces
        cv, dv = self.cv, self.dv
        ld, sv, ss = self.S("cn_ld"), self.S("cn_v"), self.S("cn_s")
        with nc.Block() as blk:
            @blk.sync
            def _(sync):
                sync.dma_start(out=cv[:], in_=cvec_ap).then_inc(ld, 16)

            @blk.vector
            def _(v):
                CH = lambda ins: kbself.chain("v", v, ins)
                v.memset(self.c1024[:], 1.0 / 1024.0)
                v.memset(self.jbg[:], 1.0)
                v.memset(self.ep1[:], EPS)
                v.memset(self.mh1[:], -0.5)
                v.memset(self.one1[:], 1.0)
                v.memset(self.one_pp[:], 1.0)
                v.memset(self.half_pp[:], 0.5)
                v.memset(self.c01_pp[:], 0.1)
                v.wait_ge(ld, 16)
                tsv = cv[:, CV_TS:CV_TS + 8]
                CH(v.tensor_scalar_add(dv[:, DV_TS1:DV_TS1 + 8], tsv, 1.0))
                CH(v.tensor_scalar_mul(dv[:, DV_NEGTS:DV_NEGTS + 8], tsv, -1.0))
                CH(v.tensor_scalar(dv[:, DV_ONEMTS:DV_ONEMTS + 8], tsv, -1.0, 1.0,
                                   ALU.mult, ALU.add))
                th = cv[:, CV_TH:CV_TH + 32]
                pa = self.sc32a
                pb = self.sc128
                msk = self.sc32b
                CH(v.tensor_copy(pa[:], th))
                for _ in range(6):
                    CH(v.tensor_single_scalar(msk[:], pa[:], np.pi, ALU.is_gt))
                    CH(v.scalar_tensor_tensor(pb[:, 0:32], msk[:], -2.0 * np.pi,
                                              pa[:], ALU.mult, ALU.add))
                    CH(v.tensor_single_scalar(msk[:], pb[:, 0:32], -np.pi, ALU.is_lt))
                    CH(v.scalar_tensor_tensor(pa[:], msk[:], 2.0 * np.pi,
                                              pb[:, 0:32], ALU.mult, ALU.add))
                v.tensor_copy(self.sc32b[:], pa[:]).then_inc(sv, 1)

            @blk.scalar
            def _(s):
                CHS = lambda ins: kbself.chain("s", s, ins)
                s.wait_ge(sv, 1)
                CHS(s.activation(dv[:, DV_STH:DV_STH + 32], self.sc32a[:], AF.Sin))
                CHS(s.activation(self.sc32b[:], self.sc32a[:], AF.Sin, scale=self.half_pp[:]))
                CHS(s.activation(dv[:, DV_U:DV_U + 8], cv[:, CV_U:CV_U + 8], AF.Exp))
                s.activation(self.sc128[:], cv[:, CV_ALOG:CV_ALOG + 128],
                             AF.Exp).then_inc(ss, 1)

            @blk.vector
            def _(v):
                CH2 = lambda ins: kbself.chain("v", v, ins)
                v.wait_ge(ss, 1)
                CH2(v.tensor_scalar_mul(dv[:, DV_ANEG:DV_ANEG + 128], self.sc128[:], -1.0))
                CH2(v.tensor_mul(self.sc32a[:], self.sc32b[:], self.sc32b[:]))
                CH2(v.tensor_scalar(dv[:, DV_CTH:DV_CTH + 32], self.sc32a[:], -2.0, 1.0,
                                    ALU.mult, ALU.add))
                CH2(v.tensor_mul(self.sc32a[:], dv[:, DV_CTH:DV_CTH + 32],
                                 cv[:, CV_QWR:CV_QWR + 32]))
                CH2(v.tensor_mul(self.sc32b[:], dv[:, DV_STH:DV_STH + 32],
                                 cv[:, CV_QWI:CV_QWI + 32]))
                CH2(v.tensor_add(self.sc128[:, 0:32], self.sc32a[:], self.sc32b[:]))
                CH2(v.tensor_mul(self.sc32a[:], self.sc128[:, 0:32], cv[:, CV_QWI:CV_QWI + 32]))
                CH2(v.tensor_scalar_mul(dv[:, DV_CQP:DV_CQP + 32], self.sc32a[:], 0.1))
                CH2(v.tensor_scalar_mul(dv[:, DV_B1S:DV_B1S + 32], cv[:, CV_B1:CV_B1 + 32], 0.1))
                CH2(v.tensor_scalar(dv[:, DV_B1SP:DV_B1SP + 32], cv[:, CV_B1:CV_B1 + 32],
                                    0.1, PIH, ALU.mult, ALU.add))


def emit_phase_C(kb, x1_ap, x2_ap, w1_ap, w2_ap, bcd_ap, NCH=16, dbg=None):
    nc = kb.nc
    Q = 512
    S = kb.S
    c_w, c_in, c_sq, c_m2, c_sd, c_sa, c_sd2, c_bc = (
        S("c_w"), S("c_in"), S("c_sq"), S("c_m2"), S("c_sd"), S("c_sa"),
        S("c_sd2"), S("c_bc"))
    c_t0, c_gb, c_mmh, c_ev, c_q, c_ab, c_mmo, c_fin, c_st = (
        S("c_t0"), S("c_gb"), S("c_mmh"), S("c_ev"), S("c_q"), S("c_ab"),
        S("c_mmo"), S("c_fin"), S("c_st"))

    es = contextlib.ExitStack()
    A = lambda name, shape, dt=F32: es.enter_context(nc.sbuf_tensor(name, shape, dt))
    PSUM = lambda name, shape, dt=F32: es.enter_context(nc.psum_tensor(name, shape, dt))
    w1b = A("c_w1b", [P, CT * F], BF16)
    w2b = A("c_w2b", [P, FT * C], BF16)
    xin = A("c_xin", [P, CT * Q], F32)
    gbf = A("c_gbf", [P, CT * Q], BF16)
    abf = A("c_abf", [P, FT * Q], BF16)
    sq = [A(f"c_sq{j}", [P, Q], F32) for j in range(2)]
    tmp = sq
    s1t = [A(f"c_s1t{j}", [P, Q], BF16) for j in range(2)]
    c1t = [A(f"c_c1t{j}", [P, Q], BF16) for j in range(2)]
    mus = A("c_mus", [1, Q], F32)
    m2s = A("c_m2s", [1, Q], F32)
    vars_ = A("c_vars", [1, Q], F32)
    rstd = A("c_rstd", [1, Q], F32)
    mub = A("c_mub", [P, Q], F32)
    rb = A("c_rb", [P, Q], F32)
    ps_m1 = PSUM("c_psm1", [1, Q], F32)
    ps_m2 = PSUM("c_psm2", [1, Q], F32)
    ps_h = [PSUM(f"c_psh{j}", [P, Q], F32) for j in range(2)]
    ps_o = [PSUM(f"c_pso{j}", [P, Q], F32) for j in range(2)]
    jb = kb.jbg
    ps_junk = PSUM("c_psj", [1, 8], F32)

    x1r = x1_ap.rearrange("(ct p) t -> p ct t", p=P)
    x2r = x2_ap.rearrange("(ct p) t -> p ct t", p=P)
    w1r = w1_ap.rearrange("(dt p) f -> p dt f", p=P)
    w2r = w2_ap.rearrange("(ft p) d -> p ft d", p=P)

    cv, dv = kb.cv, kb.dv

    with nc.Block() as blk:
        @blk.sync
        def _(sync):
            for i in range(NCH):
                if i > 0:
                    sync.wait_ge(c_fin, 8 * i)
                    sync.dma_start(out=x2r[:, :, tsl(i - 1, Q)],
                                   in_=xin.ap().rearrange("p (ct q) -> p ct q", ct=CT)
                                   ).then_inc(c_st, 16)
                if i > 0:
                    sync.wait_ge(c_st, 16 * i)
                sync.dma_start(out=xin.ap().rearrange("p (ct q) -> p ct q", ct=CT),
                               in_=x1r[:, :, tsl(i, Q)]).then_inc(c_in, 16)
            sync.wait_ge(c_fin, 8 * NCH)
            sync.dma_start(out=x2r[:, :, tsl(NCH - 1, Q)],
                           in_=xin.ap().rearrange("p (ct q) -> p ct q", ct=CT)
                           ).then_inc(c_st, 16)
            sync.wait_ge(c_st, 16 * NCH)
            if dbg is not None:
                nd = 0
                sync.wait_ge(S("c_dbg"), 1)
                for name, src in (("gbf", gbf), ("abf", abf), ("mub", mub),
                                  ("rb", rb), ("s1", s1t[1]), ("c1", c1t[1]), ("w1b", w1b),
                                  ("hid", sq[0]), ("hid0", sq[1]), ("dv", kb.dv), ("cv", kb.cv)):
                    if name in dbg:
                        sync.dma_start(out=dbg[name], in_=src[:]).then_inc(c_st, 16)
                        nd += 1
                sync.wait_ge(c_st, 16 * (NCH + nd))

        @blk.gpsimd
        def _(g):
            g.dma_start(out=w1b.ap().rearrange("p (dt f) -> p dt f", dt=CT),
                        in_=w1r).then_inc(c_w, 16)
            g.dma_start(out=w2b.ap().rearrange("p (ft d) -> p ft d", ft=FT),
                        in_=w2r).then_inc(c_w, 16)
            for i in range(NCH):
                g.wait_ge(c_sd2, i + 1)
                for j, strip in enumerate((mus, rstd)):
                    g.dma_start(out=bcd_ap[2 * (i % 2) + j: 2 * (i % 2) + j + 1, 0:Q],
                                in_=strip[:]).then_inc(c_bc, 16)
                g.wait_ge(c_bc, 64 * i + 32)
                for j, dst in enumerate((mub, rb)):
                    row = bcd_ap[2 * (i % 2) + j: 2 * (i % 2) + j + 1, 0:Q]
                    g.dma_start(out=dst[:], in_=bass.AP(row.tensor, row.offset,
                                                        [[0, P], [1, Q]])).then_inc(c_bc, 16)

        @blk.scalar
        def _(s):
            for i in range(NCH):
                for ct in range(CT):
                    s.wait_ge(c_in, 16 * (i + 1))
                    if i * 8 + ct >= 2:
                        s.wait_ge(c_m2, 8 * i + ct - 1)
                    s.activation(sq[ct % 2][:], xin[:, tsl(ct, Q)], AF.Square
                                 ).then_inc(c_sq, 1)
                s.wait_ge(c_sd, i + 1)
                kb.chain("s", s, s.activation(m2s[:], vars_[:], AF.Ln, bias=kb.ep1[:]))
                s.activation(rstd[:], m2s[:], AF.Exp, scale=kb.mh1[:]).then_inc(c_sa, 1)
                for ft in range(FT):
                    s.wait_ge(c_mmh, FT * i + ft + 1)
                    if FT * i + ft >= 2:
                        s.wait_ge(c_q, FT * i + ft - 1)
                    s.activation(s1t[ft % 2][:], ps_h[ft % 2][:], AF.Sin,
                                 scale=kb.c01_pp[:], bias=dv[:, DV_B1S + ft:DV_B1S + ft + 1])
                    s.activation(c1t[ft % 2][:], ps_h[ft % 2][:], AF.Sin,
                                 scale=kb.c01_pp[:], bias=dv[:, DV_B1SP + ft:DV_B1SP + ft + 1]
                                 ).then_inc(c_ev, 2)
                    if ft >= 1:
                        s.wait_ge(c_q, FT * i + ft)
                        s.activation(abf[:, tsl(ft - 1, Q)], c1t[(ft - 1) % 2][:],
                                     AF.Relu, bias=cv[:, CV_B1 + ft - 1:CV_B1 + ft]
                                     ).then_inc(c_ab, 1)
                s.wait_ge(c_q, FT * (i + 1))
                s.activation(abf[:, tsl(FT - 1, Q)], c1t[(FT - 1) % 2][:],
                             AF.Relu, bias=cv[:, CV_B1 + FT - 1:CV_B1 + FT]
                             ).then_inc(c_ab, 1)
            if dbg is not None:
                s.wait_ge(c_fin, 8 * NCH)
                s.activation(sq[0][:], ps_h[1][:], AF.Copy)
                s.activation(sq[1][:], ps_h[0][:], AF.Copy).then_inc(S("c_dbg"), 1)

        @blk.tensor
        def _(t):
            t.wait_ge(S("c_ji"), 1)
            for i in range(NCH):
                t.wait_ge(c_in, 16 * (i + 1))
                if i >= 1:
                    t.wait_ge(c_sd, i)
                for ct in range(CT):
                    t.matmul(ps_m1[:], kb.c1024[:], xin[:, tsl(ct, Q)],
                             start=(ct == 0), stop=(ct == CT - 1))
                for ct in range(CT):
                    t.wait_ge(c_sq, 8 * i + ct + 1)
                    t.matmul(ps_m2[:], kb.c1024[:], sq[ct % 2][:],
                             start=(ct == 0), stop=(ct == CT - 1))
                    t.matmul(ps_junk[:], jb[:, 0:1], jb[:, 0:8],
                             start=True, stop=True).then_inc(c_m2, 1)
                t.wait_ge(c_w, 32)
                t.wait_ge(c_gb, 8 * (i + 1))
                for ft in range(FT):
                    if FT * i + ft >= 2:
                        t.wait_ge(c_q, FT * i + ft - 1)
                    for ct in range(CT):
                        t.matmul(ps_h[ft % 2][:], w1b[:, ct * F + ft * P: ct * F + (ft + 1) * P],
                                  gbf[:, tsl(ct, Q)], start=(ct == 0), stop=(ct == CT - 1))
                    t.matmul(ps_junk[:], jb[:, 0:1], jb[:, 0:8],
                             start=True, stop=True).then_inc(c_mmh, 1)
                t.wait_ge(c_ab, FT * (i + 1))
                for dt in range(CT):
                    if 8 * i + dt >= 2:
                        t.wait_ge(c_fin, 8 * i + dt - 1)
                    for ft in range(FT):
                        t.matmul(ps_o[dt % 2][:], w2b[:, ft * C + dt * P: ft * C + (dt + 1) * P],
                                  abf[:, tsl(ft, Q)], start=(ft == 0), stop=(ft == FT - 1))
                    t.matmul(ps_junk[:], jb[:, 0:1], jb[:, 0:8],
                             start=True, stop=True).then_inc(c_mmo, 1)

        @blk.vector
        def _(v):
            v.memset(jb[:], 1.0).then_inc(S("c_ji"), 1)
            for i in range(NCH):
                v.wait_ge(c_m2, 8 * (i + 1))
                CHV = lambda ins: kb.chain("v", v, ins)
                CHV(v.tensor_copy(mus[:], ps_m1[:]))
                CHV(v.tensor_copy(m2s[:], ps_m2[:]))
                CHV(v.tensor_mul(vars_[:], mus[:], mus[:]))
                v.tensor_sub(vars_[:], m2s[:], vars_[:]).then_inc(c_sd, 1)
                v.wait_ge(c_sa, i + 1)
                v.tensor_mul(mus[:], mus[:], rstd[:]).then_inc(c_sd2, 1)
                v.wait_ge(c_bc, 64 * (i + 1))
                for ct in range(CT):
                    v.tensor_mul(tmp[ct % 2][:], xin[:, tsl(ct, Q)], rb[:])
                    v.tensor_sub(gbf[:, tsl(ct, Q)], tmp[ct % 2][:], mub[:]
                                 ).then_inc(c_gb, 1)
                for ft in range(FT):
                    v.wait_ge(c_ev, 64 * i + 2 * ft + 2)
                    if FT * i + ft >= 2:
                        v.wait_ge(c_ab, FT * i + ft - 1)
                    v.tensor_scalar_mul(abf[:, tsl(ft, Q)], s1t[ft % 2][:],
                                        dv[:, DV_CTH + ft:DV_CTH + ft + 1])
                    v.scalar_tensor_tensor(s1t[ft % 2][:], c1t[ft % 2][:],
                                           dv[:, DV_STH + ft:DV_STH + ft + 1],
                                           abf[:, tsl(ft, Q)], ALU.mult, ALU.add)
                    v.scalar_tensor_tensor(c1t[ft % 2][:], s1t[ft % 2][:],
                                           dv[:, DV_CQP + ft:DV_CQP + ft + 1],
                                           ps_h[ft % 2][:], ALU.mult, ALU.add
                                           ).then_inc(c_q, 1)
                for dt in range(CT):
                    v.wait_ge(c_mmo, 8 * i + dt + 1)
                    v.scalar_tensor_tensor(xin[:, tsl(dt, Q)], ps_o[dt % 2][:],
                                           cv[:, CV_B2 + dt:CV_B2 + dt + 1],
                                           xin[:, tsl(dt, Q)], ALU.add, ALU.add
                                           ).then_inc(c_fin, 1)


    return es

def emit_phase_E(kb, x2_ap, dpart_ap, x4_ap, wfc_ap, wproj_ap, bcd_ap, NCH=32):
    nc = kb.nc
    Q = 256
    S = kb.S
    e_w, e_in, e_sq, e_m2, e_sd, e_sa, e_sd2, e_bc = (
        S("e_w"), S("e_in"), S("e_sq"), S("e_m2"), S("e_sd"), S("e_sa"),
        S("e_sd2"), S("e_bc"))
    e_t0, e_gb, e_mmh, e_ev, e_q2, e_mmo, e_spk, e_fin, e_st = (
        S("e_t0"), S("e_gb"), S("e_mmh"), S("e_ev"), S("e_q2"), S("e_mmo"),
        S("e_spk"), S("e_fin"), S("e_st"))

    es = contextlib.ExitStack()
    A = lambda name, shape, dt=F32: es.enter_context(nc.sbuf_tensor(name, shape, dt))
    PSUM = lambda name, shape, dt=F32: es.enter_context(nc.psum_tensor(name, shape, dt))
    wfb = A("e_wfb", [P, CT * F], BF16)
    wpb = A("e_wpb", [P, FT * C], BF16)
    xin = A("e_xin", [P, CT * Q], F32)
    gbf = A("e_gbf", [P, CT * Q], BF16)
    hbf = A("e_hbf", [P, FT * Q], BF16)
    ysb = A("e_ysb", [P, CT * Q], F32)
    acc = A("e_acc", [P, CT * Q], F32)
    sq = [A(f"e_sq{j}", [P, Q], F32) for j in range(2)]
    st_ = [A(f"e_st{j}", [P, Q], F32) for j in range(2)]
    mem = [A(f"e_mem{j}", [P, Q], F32) for j in range(2)]
    spk = A("e_spk", [P, Q], F32)
    mus = A("e_mus", [1, Q], F32)
    m2s = A("e_m2s", [1, Q], F32)
    vars_ = A("e_vars", [1, Q], F32)
    rstd = A("e_rstd", [1, Q], F32)
    mub = A("e_mub", [P, Q], F32)
    rb = A("e_rb", [P, Q], F32)
    ps_m1 = PSUM("e_psm1", [1, Q], F32)
    ps_m2 = PSUM("e_psm2", [1, Q], F32)
    ps_h = [PSUM(f"e_psh{j}", [P, Q], F32) for j in range(2)]
    ps_o = [PSUM(f"e_pso{j}", [P, Q], F32) for j in range(2)]
    jb = kb.jbg
    ps_junk = PSUM("e_psj", [1, 8], F32)

    x2r = x2_ap.rearrange("(ct p) t -> p ct t", p=P)
    dpr = dpart_ap.rearrange("(ct p) t -> p ct t", p=P)
    x4r = x4_ap.rearrange("(ct p) t -> p ct t", p=P)
    wfr = wfc_ap.rearrange("(dt p) f -> p dt f", p=P)
    wpr = wproj_ap.rearrange("(ft p) d -> p ft d", p=P)
    cv, dv = kb.cv, kb.dv

    with nc.Block() as blk:
        @blk.sync
        def _(sync):
            for i in range(NCH):
                if i > 0:
                    sync.wait_ge(e_spk, 8 * i)
                    sync.dma_start(out=x4r[:, :, tsl(i - 1, Q)],
                                   in_=xin.ap().rearrange("p (ct q) -> p ct q", ct=CT)
                                   ).then_inc(e_st, 16)
                if i > 0:
                    sync.wait_ge(e_st, 16 * i)
                sync.dma_start(out=ysb.ap().rearrange("p (ct q) -> p ct q", ct=CT),
                               in_=dpr[:, :, tsl(i, Q)]).then_inc(e_in, 16)
                sync.dma_start(out=xin.ap().rearrange("p (ct q) -> p ct q", ct=CT),
                               in_=x2r[:, :, tsl(i, Q)]).then_inc(e_in, 16)
            sync.wait_ge(e_spk, 8 * NCH)
            sync.dma_start(out=x4r[:, :, tsl(NCH - 1, Q)],
                           in_=xin.ap().rearrange("p (ct q) -> p ct q", ct=CT)
                           ).then_inc(e_st, 16)
            sync.wait_ge(e_st, 16 * NCH)

        @blk.gpsimd
        def _(g):
            g.dma_start(out=wfb.ap().rearrange("p (dt f) -> p dt f", dt=CT),
                        in_=wfr).then_inc(e_w, 16)
            g.dma_start(out=wpb.ap().rearrange("p (ft d) -> p ft d", ft=FT),
                        in_=wpr).then_inc(e_w, 16)
            for i in range(NCH):
                g.wait_ge(e_sd2, i + 1)
                for j, strip in enumerate((mus, rstd)):
                    g.dma_start(out=bcd_ap[2 * (i % 2) + j: 2 * (i % 2) + j + 1, 0:Q],
                                in_=strip[:]).then_inc(e_bc, 16)
                g.wait_ge(e_bc, 64 * i + 32)
                for j, dst in enumerate((mub, rb)):
                    row = bcd_ap[2 * (i % 2) + j: 2 * (i % 2) + j + 1, 0:Q]
                    g.dma_start(out=dst[:], in_=bass.AP(row.tensor, row.offset,
                                                        [[0, P], [1, Q]])).then_inc(e_bc, 16)
                g.wait_ge(e_bc, 64 * (i + 1))
                for ct in range(CT):
                    if 8 * i + ct >= 2:
                        g.wait_ge(e_gb, 8 * i + ct - 1)
                    g.tensor_mul(sq[ct % 2][:], xin[:, tsl(ct, Q)], rb[:]
                                 ).then_inc(e_t0, 1)

        @blk.scalar
        def _(s):
            for i in range(NCH):
                s.wait_ge(e_spk, 8 * i)
                for ct in range(CT):
                    s.wait_ge(e_in, 32 * (i + 1))
                    s.wait_ge(e_fin, 8 * i + ct + 1)
                    if i * 8 + ct >= 2:
                        s.wait_ge(e_m2, 8 * i + ct - 1)
                    s.activation(sq[ct % 2][:], xin[:, tsl(ct, Q)], AF.Square
                                 ).then_inc(e_sq, 1)
                s.wait_ge(e_sd, i + 1)
                kb.chain("s", s, s.activation(m2s[:], vars_[:], AF.Ln, bias=kb.ep1[:]))
                s.activation(rstd[:], m2s[:], AF.Exp, scale=kb.mh1[:]).then_inc(e_sa, 1)
                for ht in range(FT):
                    s.wait_ge(e_mmh, FT * i + ht + 1)
                    if FT * i + ht >= 2:
                        s.wait_ge(e_q2, FT * i + ht - 1)
                    kb.chain("s", s, s.activation(st_[ht % 2][:], ps_h[ht % 2][:], AF.Relu))
                    s.activation(hbf[:, tsl(ht, Q)], st_[ht % 2][:], AF.Square
                                 ).then_inc(e_q2, 1)
                for dt in range(CT):
                    s.wait_ge(e_mmo, 8 * i + dt + 1)
                    s.activation(ysb[:, tsl(dt, Q)], ps_o[dt % 2][:], AF.Copy
                                 ).then_inc(e_ev, 1)

        @blk.tensor
        def _(t):
            for i in range(NCH):
                t.wait_ge(e_sq, 8 * i)
                t.wait_ge(e_fin, 8 * (i + 1) - 0)
                if i >= 1:
                    t.wait_ge(e_sd, i)
                for ct in range(CT):
                    t.matmul(ps_m1[:], kb.c1024[:], xin[:, tsl(ct, Q)],
                             start=(ct == 0), stop=(ct == CT - 1))
                for ct in range(CT):
                    t.wait_ge(e_sq, 8 * i + ct + 1)
                    t.matmul(ps_m2[:], kb.c1024[:], sq[ct % 2][:],
                             start=(ct == 0), stop=(ct == CT - 1))
                    t.matmul(ps_junk[:], jb[:, 0:1], jb[:, 0:8],
                             start=True, stop=True).then_inc(e_m2, 1)
                t.wait_ge(e_w, 32)
                t.wait_ge(e_gb, 8 * (i + 1))
                for ht in range(FT):
                    if FT * i + ht >= 2:
                        t.wait_ge(e_q2, FT * i + ht - 1)
                    for ct in range(CT):
                        t.matmul(ps_h[ht % 2][:], wfb[:, ct * F + ht * P: ct * F + (ht + 1) * P],
                                 gbf[:, tsl(ct, Q)], start=(ct == 0), stop=(ct == CT - 1))
                    t.matmul(ps_junk[:], jb[:, 0:1], jb[:, 0:8],
                             start=True, stop=True).then_inc(e_mmh, 1)
                t.wait_ge(e_q2, FT * (i + 1))
                for dt in range(CT):
                    if 8 * i + dt >= 2:
                        t.wait_ge(e_ev, 8 * i + dt - 1)
                    for ht in range(FT):
                        t.matmul(ps_o[dt % 2][:], wpb[:, ht * C + dt * P: ht * C + (dt + 1) * P],
                                 hbf[:, tsl(ht, Q)], start=(ht == 0), stop=(ht == FT - 1))
                    t.matmul(ps_junk[:], jb[:, 0:1], jb[:, 0:8],
                             start=True, stop=True).then_inc(e_mmo, 1)

        @blk.vector
        def _(v):
            for i in range(NCH):
                v.wait_ge(e_in, 32 * (i + 1))
                if i > 0:
                    v.wait_ge(e_st, 16 * i)
                for ct in range(CT):
                    v.tensor_add(xin[:, tsl(ct, Q)], xin[:, tsl(ct, Q)],
                                 ysb[:, tsl(ct, Q)]).then_inc(e_fin, 1)
                v.wait_ge(e_m2, 8 * (i + 1))
                CHV = lambda ins: kb.chain("v", v, ins)
                CHV(v.tensor_copy(mus[:], ps_m1[:]))
                CHV(v.tensor_copy(m2s[:], ps_m2[:]))
                CHV(v.tensor_mul(vars_[:], mus[:], mus[:]))
                v.tensor_sub(vars_[:], m2s[:], vars_[:]).then_inc(e_sd, 1)
                v.wait_ge(e_sa, i + 1)
                v.tensor_mul(mus[:], mus[:], rstd[:]).then_inc(e_sd2, 1)
                for ct in range(CT):
                    v.wait_ge(e_t0, 8 * i + ct + 1)
                    v.tensor_sub(gbf[:, tsl(ct, Q)], sq[ct % 2][:], mub[:]
                                 ).then_inc(e_gb, 1)
                CHV2 = lambda ins: kb.chain("v", v, ins)
                for dt in range(CT):
                    v.wait_ge(e_ev, 8 * i + dt + 1)
                    y = ysb[:, tsl(dt, Q)]
                    a_ = acc[:, tsl(dt, Q)]
                    CHV2(v.tensor_single_scalar(a_, y, 1.0, ALU.is_gt))
                    mprev = y
                    for k in range(4):
                        mb_ = mem[k % 2]
                        CHV2(v.tensor_single_scalar(spk[:], mprev, 1.0, ALU.is_gt))
                        CHV2(v.scalar_tensor_tensor(mb_[:], mprev, 0.9, y, ALU.mult, ALU.add))
                        CHV2(v.tensor_sub(mb_[:], mb_[:], spk[:]))
                        CHV2(v.tensor_single_scalar(spk[:], mb_[:], 1.0, ALU.is_gt))
                        CHV2(v.tensor_add(a_, a_, spk[:]))
                        mprev = mb_[:]
                    v.scalar_tensor_tensor(xin[:, tsl(dt, Q)], a_, 0.2,
                                           xin[:, tsl(dt, Q)], ALU.mult, ALU.add
                                           ).then_inc(e_spk, 1)


    return es

def emit_phase_B(kb, x_ap, a_ap, x1_ap, wck_ap, wcv_ap, wcr_ap, bcd_ap, NCH=32):
    nc = kb.nc
    Q = 256
    S = kb.S
    b_w, b_in, b_sq, b_m2, b_sd, b_sa, b_sd2, b_bc = (
        S("b_w"), S("b_in"), S("b_sq"), S("b_m2"), S("b_sd"), S("b_sa"),
        S("b_sd2"), S("b_bc"))
    b_xx, b_q2, b_mmk, b_sig, b_mmr, b_mmo, b_g1, b_xs, b_st, b_wu = (
        S("b_xx"), S("b_q2"), S("b_mmk"), S("b_sig"), S("b_mmr"), S("b_mmo"),
        S("b_g1"), S("b_xs"), S("b_st"), S("b_wu"))

    es = contextlib.ExitStack()
    A = lambda name, shape, dt=F32: es.enter_context(nc.sbuf_tensor(name, shape, dt))
    PSUM = lambda name, shape, dt=F32: es.enter_context(nc.psum_tensor(name, shape, dt))
    wckb = A("b_wckb", [P, CT * F], BF16)
    wcvb = A("b_wcvb", [P, FT * C], BF16)
    wcrb = A("b_wcrb", [P, CT * C], BF16)
    ain = A("b_ain", [P, CT * Q], F32)
    xin = A("b_xin", [P, CT * Q], F32)
    xxx2b = A("b_xxx2b", [P, CT * Q], BF16)
    q2 = A("b_q2t", [P, FT * Q], BF16)
    rc = A("b_rc", [P, CT * Q], F32)
    h2t = [A(f"b_h2t{j}", [P, Q + 16], F32) for j in range(2)]
    sq = [A(f"b_sqt{j}", [P, Q], F32) for j in range(2)]
    st_ = [A(f"b_st{j}", [P, Q], F32) for j in range(2)]
    g1t = [A(f"b_g1t{j}", [P, Q], F32) for j in range(2)]
    h2last = A("b_h2last", [P, 8], F32)
    colbuf = A("b_colbuf", [P, 8], F32)
    sqcol = A("b_sqcol", [P, 8], F32)
    mu1b = A("b_mu1b", [P, 1], F32)
    r1b = A("b_r1b", [P, 1], F32)
    w1s = A("b_w1s", [1, 8], F32)
    mus = A("b_mus", [1, Q], F32)
    m2s = A("b_m2s", [1, Q], F32)
    vars_ = A("b_vars", [1, Q], F32)
    rstd = A("b_rstd", [1, Q], F32)
    mub = A("b_mub", [P, Q], F32)
    rb = A("b_rb", [P, Q], F32)
    ps_m1 = PSUM("b_psm1", [1, Q], F32)
    ps_m2 = PSUM("b_psm2", [1, Q], F32)
    ps_k = [PSUM(f"b_psk{j}", [P, Q], F32) for j in range(2)]
    ps_v = [PSUM(f"b_psv{j}", [P, Q], F32) for j in range(2)]
    ps_r = PSUM("b_psr", [P, Q], F32)
    jb = kb.jbg
    ps_junk = PSUM("b_psj", [1, 8], F32)

    xr = x_ap.rearrange("(ct p) t -> p ct t", p=P)
    ar = a_ap.rearrange("(ct p) t -> p ct t", p=P)
    x1r = x1_ap.rearrange("(ct p) t -> p ct t", p=P)
    wckr = wck_ap.rearrange("(ct p) o -> p ct o", p=P)
    wcvr = wcv_ap.rearrange("(ht p) d -> p ht d", p=P)
    wcrr = wcr_ap.rearrange("(ct p) o -> p ct o", p=P)
    cv, dv = kb.cv, kb.dv

    with nc.Block() as blk:
        @blk.sync
        def _(sync):
            for ct in range(CT):
                sync.dma_start(out=colbuf[:, ct:ct + 1], in_=ar[:, ct, T - 1:T],
                               allow_slow_non_contiguous=True).then_inc(b_wu, 16)
            for i in range(NCH):
                if i > 0:
                    sync.wait_ge(b_g1, 8 * i)
                    sync.dma_start(out=x1r[:, :, tsl(i - 1, Q)],
                                   in_=xin.ap().rearrange("p (ct q) -> p ct q", ct=CT)
                                   ).then_inc(b_st, 16)
                if i > 0:
                    sync.wait_ge(b_st, 16 * i)
                sync.dma_start(out=ain.ap().rearrange("p (ct q) -> p ct q", ct=CT),
                               in_=ar[:, :, tsl(i, Q)]).then_inc(b_in, 16)
                sync.dma_start(out=xin.ap().rearrange("p (ct q) -> p ct q", ct=CT),
                               in_=xr[:, :, tsl(i, Q)]).then_inc(b_in, 16)
            sync.wait_ge(b_g1, 8 * NCH)
            sync.dma_start(out=x1r[:, :, tsl(NCH - 1, Q)],
                           in_=xin.ap().rearrange("p (ct q) -> p ct q", ct=CT)
                           ).then_inc(b_st, 16)
            sync.wait_ge(b_st, 16 * NCH)

        @blk.gpsimd
        def _(g):
            g.dma_start(out=wckb.ap().rearrange("p (ct o) -> p ct o", ct=CT),
                        in_=wckr).then_inc(b_w, 16)
            g.dma_start(out=wcvb.ap().rearrange("p (ht d) -> p ht d", ht=FT),
                        in_=wcvr).then_inc(b_w, 16)
            g.dma_start(out=wcrb.ap().rearrange("p (ct o) -> p ct o", ct=CT),
                        in_=wcrr).then_inc(b_w, 16)
            g.wait_ge(b_sd2, 1)
            g.dma_start(out=bcd_ap[0:1, 0:2], in_=w1s[0:1, 0:2]).then_inc(b_bc, 16)
            g.wait_ge(b_bc, 16)
            r0 = bcd_ap[0:1, 0:1]
            g.dma_start(out=mu1b[:], in_=bass.AP(r0.tensor, r0.offset, [[0, P], [1, 1]])
                        ).then_inc(b_bc, 16)
            r1_ = bcd_ap[0:1, 1:2]
            g.dma_start(out=r1b[:], in_=bass.AP(r1_.tensor, r1_.offset, [[0, P], [1, 1]])
                        ).then_inc(b_bc, 16)
            for i in range(NCH):
                g.wait_ge(b_sd2, i + 2)
                for j, strip in enumerate((mus, rstd)):
                    g.dma_start(out=bcd_ap[2 * (i % 2) + 1 + j: 2 * (i % 2) + 2 + j, 0:Q],
                                in_=strip[:]).then_inc(b_bc, 16)
                g.wait_ge(b_bc, 64 * i + 48 + 32)
                for j, dst in enumerate((mub, rb)):
                    row = bcd_ap[2 * (i % 2) + 1 + j: 2 * (i % 2) + 2 + j, 0:Q]
                    g.dma_start(out=dst[:], in_=bass.AP(row.tensor, row.offset,
                                                        [[0, P], [1, Q]])).then_inc(b_bc, 16)
                g.wait_ge(b_bc, 64 * (i + 1) + 48)
                for ct in range(CT):
                    g.wait_ge(b_xx, 8 * i + ct + 1)
                    g.tensor_add(ain[:, tsl(ct, Q)], ain[:, tsl(ct, Q)],
                                 xin[:, tsl(ct, Q)]).then_inc(b_xs, 1)

        @blk.scalar
        def _(s):
            s.wait_ge(b_wu, 16 * CT)
            s.activation(sqcol[:], colbuf[:], AF.Square).then_inc(S("b_wsq"), 1)
            s.wait_ge(b_sd, 1)
            kb.chain("s", s, s.activation(w1s[0:1, 2:3], w1s[0:1, 4:5], AF.Ln, bias=kb.ep1[:]))
            s.activation(w1s[0:1, 1:2], w1s[0:1, 2:3], AF.Exp, scale=kb.mh1[:]
                         ).then_inc(b_sa, 1)
            for i in range(NCH):
                for ct in range(CT):
                    s.wait_ge(b_in, 32 * (i + 1))
                    if i * 8 + ct >= 2:
                        s.wait_ge(b_m2, 8 * i + ct - 1)
                    s.activation(sq[ct % 2][:], ain[:, tsl(ct, Q)], AF.Square
                                 ).then_inc(b_sq, 1)
                s.wait_ge(b_sd, i + 2)
                kb.chain("s", s, s.activation(m2s[:], vars_[:], AF.Ln, bias=kb.ep1[:]))
                s.activation(rstd[:], m2s[:], AF.Exp, scale=kb.mh1[:]).then_inc(b_sa, 1)
                for ot in range(FT):
                    s.wait_ge(b_mmk, FT * i + ot + 1)
                    if FT * i + ot >= 2:
                        s.wait_ge(b_q2, FT * i + ot - 1)
                    kb.chain("s", s, s.activation(st_[ot % 2][:], ps_k[ot % 2][:], AF.Relu))
                    s.activation(q2[:, tsl(ot, Q)], st_[ot % 2][:], AF.Square
                                 ).then_inc(b_q2, 1)
                for ot in range(CT):
                    s.wait_ge(b_mmr, 8 * i + ot + 1)
                    s.activation(rc[:, tsl(ot, Q)], ps_r[:], AF.Sigmoid).then_inc(b_sig, 1)

        @blk.tensor
        def _(t):
            t.wait_ge(b_wu, 16 * CT)
            t.matmul(ps_m1[0:1, 0:8], kb.c1024[:], colbuf[:], start=True, stop=True)
            t.wait_ge(S("b_wsq"), 1)
            t.matmul(ps_m1[0:1, 8:16], kb.c1024[:], sqcol[:], start=True, stop=True)
            t.matmul(ps_junk[:], jb[:, 0:1], jb[:, 0:8], start=True, stop=True
                     ).then_inc(b_m2, 1)
            for i in range(NCH):
                t.wait_ge(b_in, 32 * (i + 1))
                t.wait_ge(b_sd, i + 1)
                for ct in range(CT):
                    t.matmul(ps_m1[:], kb.c1024[:], ain[:, tsl(ct, Q)],
                             start=(ct == 0), stop=(ct == CT - 1))
                for ct in range(CT):
                    t.wait_ge(b_sq, 8 * i + ct + 1)
                    t.matmul(ps_m2[:], kb.c1024[:], sq[ct % 2][:],
                             start=(ct == 0), stop=(ct == CT - 1))
                    t.matmul(ps_junk[:], jb[:, 0:1], jb[:, 0:8],
                             start=True, stop=True).then_inc(b_m2, 1)
                t.wait_ge(b_w, 48)
                t.wait_ge(b_xx, 8 * (i + 1))
                for ot in range(FT):
                    if FT * i + ot >= 2:
                        t.wait_ge(b_q2, FT * i + ot - 1)
                    for ct in range(CT):
                        t.matmul(ps_k[ot % 2][:], wckb[:, ct * F + ot * P: ct * F + (ot + 1) * P],
                                 xxx2b[:, tsl(ct, Q)], start=(ct == 0), stop=(ct == CT - 1))
                    t.matmul(ps_junk[:], jb[:, 0:1], jb[:, 0:8],
                             start=True, stop=True).then_inc(b_mmk, 1)
                for ot in range(CT):
                    if 8 * i + ot >= 1:
                        t.wait_ge(b_sig, 8 * i + ot)
                    for ct in range(CT):
                        t.matmul(ps_r[:], wcrb[:, ct * C + ot * P: ct * C + (ot + 1) * P],
                                 xxx2b[:, tsl(ct, Q)], start=(ct == 0), stop=(ct == CT - 1))
                    t.matmul(ps_junk[:], jb[:, 0:1], jb[:, 0:8],
                             start=True, stop=True).then_inc(b_mmr, 1)
                t.wait_ge(b_q2, FT * (i + 1))
                for dt in range(CT):
                    if 8 * i + dt >= 2:
                        t.wait_ge(b_g1, 8 * i + dt - 1)
                    for ht in range(FT):
                        t.matmul(ps_v[dt % 2][:], wcvb[:, ht * C + dt * P: ht * C + (dt + 1) * P],
                                 q2[:, tsl(ht, Q)], start=(ht == 0), stop=(ht == FT - 1))
                    t.matmul(ps_junk[:], jb[:, 0:1], jb[:, 0:8],
                             start=True, stop=True).then_inc(b_mmo, 1)

        @blk.vector
        def _(v):
            CHV = lambda ins: kb.chain("v", v, ins)
            v.wait_ge(b_m2, 1)
            CHV(v.reduce_sum(w1s[0:1, 3:4], ps_m1[0:1, 0:8], axis=mybir.AxisListType.X))
            CHV(v.reduce_sum(w1s[0:1, 5:6], ps_m1[0:1, 8:16], axis=mybir.AxisListType.X))
            CHV(v.tensor_mul(w1s[0:1, 6:7], w1s[0:1, 3:4], w1s[0:1, 3:4]))
            v.tensor_sub(w1s[0:1, 4:5], w1s[0:1, 5:6], w1s[0:1, 6:7]).then_inc(b_sd, 1)
            v.wait_ge(b_sa, 1)
            CHV(v.tensor_mul(w1s[0:1, 0:1], w1s[0:1, 3:4], w1s[0:1, 1:2]))
            v.tensor_copy(w1s[0:1, 6:7], w1s[0:1, 0:1]).then_inc(b_sd2, 1)
            v.wait_ge(b_bc, 48)
            for ct in range(CT):
                CHV(v.tensor_sub(h2t[0][:, 0:1], colbuf[:, ct:ct + 1], mu1b[:]))
                CHV(v.tensor_mul(h2t[0][:, 1:2], h2t[0][:, 0:1], r1b[:]))
                CHV(v.tensor_scalar(h2last[:, ct:ct + 1], h2t[0][:, 1:2],
                                    cv[:, CV_LN2W + ct:CV_LN2W + ct + 1],
                                    cv[:, CV_LN2B + ct:CV_LN2B + ct + 1],
                                    ALU.mult, ALU.add))
            for i in range(NCH):
                v.wait_ge(b_m2, 8 * (i + 1) + 1)
                CHV(v.tensor_copy(mus[:], ps_m1[:]))
                CHV(v.tensor_copy(m2s[:], ps_m2[:]))
                CHV(v.tensor_mul(vars_[:], mus[:], mus[:]))
                v.tensor_sub(vars_[:], m2s[:], vars_[:]).then_inc(b_sd, 1)
                v.wait_ge(b_sa, i + 2)
                v.tensor_mul(mus[:], mus[:], rstd[:]).then_inc(b_sd2, 1)
                v.wait_ge(b_bc, 64 * (i + 1) + 48)
                for ct in range(CT):
                    hb = h2t[ct % 2]
                    CHV(v.tensor_tensor(hb[:, 15:16], h2last[:, ct:ct + 1],
                                        h2last[:, ct:ct + 1], ALU.max))
                    CHV(v.tensor_mul(sq[ct % 2][:], ain[:, tsl(ct, Q)], rb[:]))
                    CHV(v.tensor_sub(st_[ct % 2][:], sq[ct % 2][:], mub[:]))
                    CHV(v.tensor_scalar(hb[:, 16:Q + 16], st_[ct % 2][:],
                                        cv[:, CV_LN2W + ct:CV_LN2W + ct + 1],
                                        cv[:, CV_LN2B + ct:CV_LN2B + ct + 1],
                                        ALU.mult, ALU.add), )
                    CHV(v.tensor_tensor(h2last[:, ct:ct + 1], hb[:, Q + 15:Q + 16],
                                        hb[:, Q + 15:Q + 16], ALU.max))
                    CHV(v.tensor_scalar_mul(g1t[ct % 2][:], hb[:, 16:Q + 16],
                                            dv[:, DV_ONEMTS + ct:DV_ONEMTS + ct + 1]))
                    v.scalar_tensor_tensor(xxx2b[:, tsl(ct, Q)], hb[:, 15:Q + 15],
                                           cv[:, CV_TS + ct:CV_TS + ct + 1],
                                           g1t[ct % 2][:], ALU.mult, ALU.add
                                           ).then_inc(b_xx, 1)
                for dt in range(CT):
                    v.wait_ge(b_mmo, 8 * i + dt + 1)
                    v.wait_ge(b_sig, 8 * (i + 1))
                    CHV(v.tensor_mul(g1t[dt % 2][:], rc[:, tsl(dt, Q)], ps_v[dt % 2][:]))
                    v.wait_ge(b_xs, 8 * i + dt + 1)
                    v.tensor_add(xin[:, tsl(dt, Q)], ain[:, tsl(dt, Q)],
                                 g1t[dt % 2][:]).then_inc(b_g1, 1)



    return es

def emit_phase_A(kb, x_ap, a_out_ap, wk_ap, wv_ap, wr_ap, bcd_ap, NCH=16):
    nc = kb.nc
    Q = 512
    S = kb.S
    a_w, a_in, a_sq, a_m2, a_sd, a_sa, a_sd2, a_sa2, a_sd3, a_bc = (
        S("a_w"), S("a_in"), S("a_sq"), S("a_m2"), S("a_sd"), S("a_sa"),
        S("a_sd2"), S("a_sa2"), S("a_sd3"), S("a_bc"))
    a_xx, a_mmk, a_mmv, a_mmr, a_evE, a_evr, a_evv, a_fin, a_st, _aju = (
        S("a_xx"), S("a_mmk"), S("a_mmv"), S("a_mmr"), S("a_evE"), S("a_evr"),
        S("a_evv"), S("a_fin"), S("a_st"), S("a_st"))

    es = contextlib.ExitStack()
    A = lambda name, shape, dt=F32: es.enter_context(nc.sbuf_tensor(name, shape, dt))
    PSUM = lambda name, shape, dt=F32: es.enter_context(nc.psum_tensor(name, shape, dt))
    wkb = A("a_wkb", [P, CT * C], BF16)
    wvb = A("a_wvb", [P, CT * C], BF16)
    wrb = A("a_wrb", [P, CT * C], BF16)
    xin = A("a_xin", [P, CT * Q], F32)
    xn = A("a_xn", [P, CT * Q], F32)
    hbuf = A("a_hbuf", [P, CT * (Q + 16)], F32)
    xxxb = A("a_xxxb", [P, CT * Q], BF16)
    sq = [A(f"a_sq{j}", [P, Q], F32) for j in range(2)]
    sc2 = [A(f"a_sc2{j}", [P, Q], F32) for j in range(2)]
    Ebuf = [A(f"a_E{j}", [P, Q], F32) for j in range(2)]
    rbuf = [A(f"a_r{j}", [P, Q], F32) for j in range(2)]
    Ev = A("a_Ev", [P, Q], F32)
    Ibuf = A("a_I", [P, Q], F32)
    Dprev = A("a_Dp", [P, Q], F32)
    Rm = A("a_Rm", [P, Q], F32)
    alp = A("a_alp", [P, Q], F32)
    bet = A("a_bet", [P, Q], F32)
    aab = A("a_aab", [P, Q + 16], F32)
    num = A("a_num", [P, Q], F32)
    den = A("a_den", [P, Q], F32)
    ones512 = A("a_ones", [P, Q], F32)
    carryI = A("a_cI", [P, 8], F32)
    carryA = A("a_cA", [P, 8], F32)
    mus = A("a_mus", [1, Q], F32)
    m2s = A("a_m2s", [1, Q], F32)
    vars_ = A("a_vars", [1, Q], F32)
    rstd = A("a_rstd", [1, Q], F32)
    r1s = A("a_r1s", [1, Q], F32)
    scr = A("a_scr", [1, Q], F32)
    mrsb = A("a_mrsb", [P, Q], F32)
    r0b = A("a_r0b", [P, Q], F32)
    r1b = A("a_r1b", [P, Q], F32)
    ps_m1 = PSUM("a_psm1", [1, Q], F32)
    ps_m2 = PSUM("a_psm2", [1, Q], F32)
    ps_k = PSUM("a_psk", [P, Q], F32)
    ps_v = PSUM("a_psv", [P, Q], F32)
    ps_r = PSUM("a_psr", [P, Q], F32)
    jb = kb.jbg
    ps_junk = PSUM("a_psj", [1, 8], F32)

    xr = x_ap.rearrange("(ct p) t -> p ct t", p=P)
    aor = a_out_ap.rearrange("(ct p) t -> p ct t", p=P)
    wkr = wk_ap.rearrange("(ct p) o -> p ct o", p=P)
    wvr = wv_ap.rearrange("(ct p) o -> p ct o", p=P)
    wrr = wr_ap.rearrange("(ct p) o -> p ct o", p=P)
    cv, dv = kb.cv, kb.dv

    with nc.Block() as blk:
        @blk.sync
        def _(sync):
            for i in range(NCH):
                if i > 0:
                    sync.wait_ge(a_fin, 8 * i)
                    sync.dma_start(out=aor[:, :, tsl(i - 1, Q)],
                                   in_=xin.ap().rearrange("p (ct q) -> p ct q", ct=CT)
                                   ).then_inc(a_st, 16)
                if i > 0:
                    sync.wait_ge(a_st, 16 * i)
                sync.dma_start(out=xin.ap().rearrange("p (ct q) -> p ct q", ct=CT),
                               in_=xr[:, :, tsl(i, Q)]).then_inc(a_in, 16)
            sync.wait_ge(a_fin, 8 * NCH)
            sync.dma_start(out=aor[:, :, tsl(NCH - 1, Q)],
                           in_=xin.ap().rearrange("p (ct q) -> p ct q", ct=CT)
                           ).then_inc(a_st, 16)
            sync.wait_ge(a_st, 16 * NCH)

        @blk.gpsimd
        def _(g):
            g.dma_start(out=wkb.ap().rearrange("p (ct o) -> p ct o", ct=CT),
                        in_=wkr).then_inc(a_w, 16)
            g.dma_start(out=wvb.ap().rearrange("p (ct o) -> p ct o", ct=CT),
                        in_=wvr).then_inc(a_w, 16)
            g.dma_start(out=wrb.ap().rearrange("p (ct o) -> p ct o", ct=CT),
                        in_=wrr).then_inc(a_w, 16)
            for i in range(NCH):
                g.wait_ge(a_sd3, i + 1)
                for j, strip in enumerate((mus, rstd, r1s)):
                    g.dma_start(out=bcd_ap[3 * (i % 2) + j: 3 * (i % 2) + j + 1, 0:Q],
                                in_=strip[:]).then_inc(a_bc, 16)
                g.wait_ge(a_bc, 96 * i + 48)
                for j, dst in enumerate((mrsb, r0b, r1b)):
                    row = bcd_ap[3 * (i % 2) + j: 3 * (i % 2) + j + 1, 0:Q]
                    g.dma_start(out=dst[:], in_=bass.AP(row.tensor, row.offset,
                                                        [[0, P], [1, Q]])).then_inc(a_bc, 16)

        @blk.scalar
        def _(s):
            for i in range(NCH):
                for ct in range(CT):
                    s.wait_ge(a_in, 16 * (i + 1))
                    if i * 8 + ct >= 2:
                        s.wait_ge(a_m2, 8 * i + ct - 1)
                    s.activation(sq[ct % 2][:], xin[:, tsl(ct, Q)], AF.Square
                                 ).then_inc(a_sq, 1)
                s.wait_ge(a_sd, i + 1)
                kb.chain("s", s, s.activation(scr[:], vars_[:], AF.Ln, bias=kb.ep1[:]))
                s.activation(rstd[:], scr[:], AF.Exp, scale=kb.mh1[:]).then_inc(a_sa, 1)
                s.wait_ge(a_sd2, i + 1)
                kb.chain("s", s, s.activation(scr[:], vars_[:], AF.Ln, bias=kb.ep1[:]))
                s.activation(r1s[:], scr[:], AF.Exp, scale=kb.mh1[:]).then_inc(a_sa2, 1)
                for ot in range(CT):
                    idx = 8 * i + ot
                    s.wait_ge(a_mmk, idx + 1)
                    if idx >= 2:
                        s.wait_ge(a_fin, idx - 1)
                    s.activation(Ebuf[ot % 2][:], ps_k[:], AF.Exp).then_inc(a_evE, 1)
                    s.wait_ge(a_mmr, idx + 1)
                    s.activation(rbuf[ot % 2][:], ps_r[:], AF.Sigmoid).then_inc(a_evr, 1)

        @blk.tensor
        def _(t):
            for i in range(NCH):
                t.wait_ge(a_in, 16 * (i + 1))
                if i >= 1:
                    t.wait_ge(a_sd, i)
                for ct in range(CT):
                    t.matmul(ps_m1[:], kb.c1024[:], xin[:, tsl(ct, Q)],
                             start=(ct == 0), stop=(ct == CT - 1))
                for ct in range(CT):
                    t.wait_ge(a_sq, 8 * i + ct + 1)
                    t.matmul(ps_m2[:], kb.c1024[:], sq[ct % 2][:],
                             start=(ct == 0), stop=(ct == CT - 1))
                    t.matmul(ps_junk[:], jb[:, 0:1], jb[:, 0:8],
                             start=True, stop=True).then_inc(a_m2, 1)
                t.wait_ge(a_w, 48)
                t.wait_ge(a_xx, 8 * (i + 1))
                for ot in range(CT):
                    if 8 * i + ot >= 1:
                        t.wait_ge(a_evE, 8 * i + ot)
                    for ct in range(CT):
                        t.matmul(ps_k[:], wkb[:, ct * C + ot * P: ct * C + (ot + 1) * P],
                                 xxxb[:, tsl(ct, Q)], start=(ct == 0), stop=(ct == CT - 1))
                    t.matmul(ps_junk[:], jb[:, 0:1], jb[:, 0:8],
                             start=True, stop=True).then_inc(a_mmk, 1)
                    if 8 * i + ot >= 1:
                        t.wait_ge(a_evv, 8 * i + ot)
                    for ct in range(CT):
                        t.matmul(ps_v[:], wvb[:, ct * C + ot * P: ct * C + (ot + 1) * P],
                                 xxxb[:, tsl(ct, Q)], start=(ct == 0), stop=(ct == CT - 1))
                    t.matmul(ps_junk[:], jb[:, 0:1], jb[:, 0:8],
                             start=True, stop=True).then_inc(a_mmv, 1)
                    if 8 * i + ot >= 1:
                        t.wait_ge(a_evr, 8 * i + ot)
                    for ct in range(CT):
                        t.matmul(ps_r[:], wrb[:, ct * C + ot * P: ct * C + (ot + 1) * P],
                                 xxxb[:, tsl(ct, Q)], start=(ct == 0), stop=(ct == CT - 1))
                    t.matmul(ps_junk[:], jb[:, 0:1], jb[:, 0:8],
                             start=True, stop=True).then_inc(a_mmr, 1)

        @blk.vector
        def _(v):
            CHV = lambda ins: kb.chain("v", v, ins)
            v.memset(ones512[:], 1.0)
            v.memset(carryI[:], 0.0)
            v.memset(carryA[:], 0.0)
            v.memset(aab[:, 15:16], 0.0)
            for ct in range(CT):
                CHV(v.tensor_tensor(hbuf[:, ct * (Q + 16) + 15: ct * (Q + 16) + 16],
                                    cv[:, CV_TS + ct:CV_TS + ct + 1],
                                    cv[:, CV_TS + ct:CV_TS + ct + 1], ALU.max))
            for i in range(NCH):
                v.wait_ge(a_m2, 8 * (i + 1))
                CHV(v.tensor_copy(mus[:], ps_m1[:]))
                CHV(v.tensor_copy(m2s[:], ps_m2[:]))
                CHV(v.tensor_mul(vars_[:], mus[:], mus[:]))
                v.tensor_sub(vars_[:], m2s[:], vars_[:]).then_inc(a_sd, 1)
                v.wait_ge(a_sa, i + 1)
                CHV(v.tensor_mul(m2s[:], vars_[:], rstd[:]))
                v.tensor_mul(vars_[:], m2s[:], rstd[:]).then_inc(a_sd2, 1)
                v.wait_ge(a_sa2, i + 1)
                v.tensor_mul(mus[:], mus[:], rstd[:]).then_inc(a_sd3, 1)
                v.wait_ge(a_bc, 96 * (i + 1))
                for ct in range(CT):
                    hb = hbuf[:, ct * (Q + 16):(ct + 1) * (Q + 16)]
                    if i > 0:
                        CHV(v.tensor_tensor(hb[:, 15:16], hb[:, Q + 15:Q + 16],
                                            hb[:, Q + 15:Q + 16], ALU.max))
                    CHV(v.tensor_mul(sq[ct % 2][:], xin[:, tsl(ct, Q)], r0b[:]))
                    CHV(v.tensor_sub(xn[:, tsl(ct, Q)], sq[ct % 2][:], mrsb[:]))
                    CHV(v.tensor_mul(sc2[ct % 2][:], xn[:, tsl(ct, Q)], r1b[:]))
                    CHV(v.tensor_scalar(hb[:, 16:Q + 16], sc2[ct % 2][:],
                                        cv[:, CV_LN1W + ct:CV_LN1W + ct + 1],
                                        cv[:, CV_LN1B + ct:CV_LN1B + ct + 1],
                                        ALU.mult, ALU.add))
                    CHV(v.tensor_scalar_mul(sc2[ct % 2][:], hb[:, 16:Q + 16],
                                            dv[:, DV_TS1 + ct:DV_TS1 + ct + 1]))
                    v.scalar_tensor_tensor(xxxb[:, tsl(ct, Q)], hb[:, 15:Q + 15],
                                           dv[:, DV_NEGTS + ct:DV_NEGTS + ct + 1],
                                           sc2[ct % 2][:], ALU.mult, ALU.add
                                           ).then_inc(a_xx, 1)
                for ot in range(CT):
                    idx = 8 * i + ot
                    v.wait_ge(a_evE, idx + 1)
                    v.wait_ge(a_mmv, idx + 1)
                    E = Ebuf[ot % 2]
                    v.tensor_mul(Ev[:], E[:], ps_v[:]).then_inc(a_evv, 1)
                    CHV(v.tensor_tensor_scan(Ibuf[:], ones512[:], E[:],
                                             carryI[:, ot:ot + 1], ALU.mult, ALU.add))
                    CHV(v.tensor_tensor(carryI[:, ot:ot + 1], Ibuf[:, Q - 1:Q],
                                        Ibuf[:, Q - 1:Q], ALU.max))
                    v.scalar_tensor_tensor(Dprev[:], Ibuf[:], 1.0, E[:],
                                           ALU.add, ALU.subtract)
                    v.tensor_max(num[:], Dprev[:], E[:])
                    v.reciprocal(Rm[:], num[:])
                    v.tensor_mul(alp[:], Dprev[:], Rm[:])
                    v.tensor_mul(bet[:], Ev[:], Rm[:])
                    CHV(v.tensor_tensor(aab[:, 15:16], carryA[:, ot:ot + 1],
                                        carryA[:, ot:ot + 1], ALU.max))
                    CHV(v.tensor_tensor_scan(aab[:, 16:Q + 16], alp[:], bet[:],
                                             carryA[:, ot:ot + 1], ALU.mult, ALU.add))
                    CHV(v.tensor_tensor(carryA[:, ot:ot + 1], aab[:, Q + 15:Q + 16],
                                        aab[:, Q + 15:Q + 16], ALU.max))
                    v.tensor_mul(num[:], Dprev[:], aab[:, 15:Q + 15])
                    v.scalar_tensor_tensor(num[:], Ev[:], dv[:, DV_U + ot:DV_U + ot + 1],
                                           num[:], ALU.mult, ALU.add)
                    v.scalar_tensor_tensor(den[:], E[:], dv[:, DV_U + ot:DV_U + ot + 1],
                                           Dprev[:], ALU.mult, ALU.add)
                    v.reciprocal(bet[:], den[:])
                    v.tensor_mul(num[:], num[:], bet[:])
                    v.wait_ge(a_evr, idx + 1)
                    v.tensor_mul(num[:], rbuf[ot % 2][:], num[:])
                    v.tensor_add(xin[:, tsl(ot, Q)], xn[:, tsl(ot, Q)], num[:]
                                 ).then_inc(a_fin, 1)


    return es

def emit_phase_D(kb, x2_ap, dpart_ap, wxp_ap, wdt_ap, bcdD_ap, NCH=8, dbg=None):
    nc = kb.nc
    Q = 1024
    H = 512
    S = kb.S
    d_w, d_in, d_sq, d_m2, d_sev, d_sd, d_sa, d_sd2, d_bc = (
        S("d_w"), S("d_in"), S("d_sq"), S("d_m2"), S("d_sev"), S("d_sd"),
        S("d_sa"), S("d_sd2"), S("d_bc"))
    d_ap, d_gb, d_mmx, d_evx, d_mmd, d_evd, d_dg, d_bcn, d_aex, d_scn, d_fin, d_st, d_mmg = (
        S("d_ap"), S("d_gb"), S("d_mmx"), S("d_evx"), S("d_mmd"), S("d_evd"),
        S("d_dg"), S("d_bcn"), S("d_aex"), S("d_scn"), S("d_fin"), S("d_st"), S("d_mmg"))
    d_ut = S("d_ut")

    es = contextlib.ExitStack()
    A = lambda name, shape, dt=F32: es.enter_context(nc.sbuf_tensor(name, shape, dt))
    PSUM = lambda name, shape, dt=F32: es.enter_context(nc.psum_tensor(name, shape, dt))
    wxpb = A("d_wxpb", [P, CT * 96], BF16)
    wdtb = A("d_wdtb", [64, CT * P], BF16)
    xin = A("d_xin", [P, CT * Q], F32)
    g = A("d_g", [P, CT * Q], F32)
    delta = A("d_delta", [P, CT * Q], F32)
    gbt = [A(f"d_gbt{j}", [P, H], BF16) for j in range(2)]
    at = [A(f"d_at{j}", [P, Q], F32) for j in range(2)]
    ut = [A(f"d_ut{j}", [P, Q], F32) for j in range(2)]
    yt = A("d_yt", [P, Q], F32)
    ht = [A("d_ht0", [P, Q], F32)] * 2
    Brep = [A(f"d_Brep{j}", [P, Q], F32) for j in range(2)]
    Crep = [A(f"d_Crep{j}", [P, Q], F32) for j in range(2)]
    Bsb = A("d_Bsb", [16, Q], F32)
    Csb = A("d_Csb", [16, Q], F32)
    dpre_bf = A("d_dpre", [64, Q], BF16)
    Ed = [A("d_Ed0", [P, H], F32)] * 2
    sq = [A(f"d_sqt{j}", [P, H], F32) for j in range(2)]
    hstate = A("d_hstate", [P, 128], F32)
    mus = A("d_mus", [1, H], F32)
    m2s = A("d_m2s", [1, H], F32)
    vars_ = A("d_vars", [1, H], F32)
    rstd = A("d_rstd", [1, H], F32)
    mub = A("d_mub", [P, Q], F32)
    rb = A("d_rb", [P, Q], F32)
    ps_m1 = PSUM("d_psm1", [1, H], F32)
    ps_m2 = PSUM("d_psm2", [1, H], F32)
    ps_dp = PSUM("d_psdp", [64, H], F32)
    ps_B = PSUM("d_psB", [16, H], F32)
    ps_C = PSUM("d_psC", [16, H], F32)
    ps_dr = [PSUM(f"d_psdr{j}", [P, H], F32) for j in range(2)]
    jb = kb.jbg
    ps_junk = PSUM("d_psj", [1, 8], F32)

    x2r = x2_ap.rearrange("(ct p) t -> p ct t", p=P)
    dpr = dpart_ap.rearrange("(ct p) t -> p ct t", p=P)
    wxr = wxp_ap.rearrange("(ct p) e -> p ct e", p=P)
    wdr = wdt_ap.rearrange("r (dt p) -> r dt p", p=P)
    cv, dv = kb.cv, kb.dv

    with nc.Block() as blk:
        @blk.sync
        def _(sync):
            for i in range(NCH):
                if i > 0:
                    sync.wait_ge(d_fin, 8 * i)
                    sync.dma_start(out=dpr[:, :, tsl(i - 1, Q)],
                                   in_=xin.ap().rearrange("p (ct q) -> p ct q", ct=CT)
                                   ).then_inc(d_st, 16)
                if i > 0:
                    sync.wait_ge(d_st, 16 * i)
                sync.dma_start(out=xin.ap().rearrange("p (ct q) -> p ct q", ct=CT),
                               in_=x2r[:, :, tsl(i, Q)]).then_inc(d_in, 16)
            sync.wait_ge(d_fin, 8 * NCH)
            sync.dma_start(out=dpr[:, :, tsl(NCH - 1, Q)],
                           in_=xin.ap().rearrange("p (ct q) -> p ct q", ct=CT)
                           ).then_inc(d_st, 16)
            sync.wait_ge(d_st, 16 * NCH)
            if dbg is not None:
                nd = 0
                for name, src in (("delta", delta), ("g", g), ("dg", dg),
                                  ("ht", ht[0]), ("at", at[1]), ("Bsb", Bsb), ("Csb", Csb), ("mub", mub), ("rb", rb)):
                    if name in dbg:
                        sync.dma_start(out=dbg[name], in_=src[:]).then_inc(d_st, 16)
                        nd += 1
                sync.wait_ge(d_st, 16 * (NCH + nd))

        @blk.gpsimd
        def _(g_):
            g_.dma_start(out=wxpb.ap().rearrange("p (ct e) -> p ct e", ct=CT),
                         in_=wxr).then_inc(d_w, 16)
            g_.dma_start(out=wdtb.ap().rearrange("r (dt p) -> r dt p", dt=CT),
                         in_=wdr).then_inc(d_w, 16)
            for i in range(NCH):
                for hf in range(2):
                    g_.wait_ge(d_sd2, 2 * i + hf + 1)
                    r0 = 36 + 4 * (i % 2) + 2 * hf
                    for j, strip in enumerate((mus, rstd)):
                        g_.dma_start(out=bcdD_ap[r0 + j: r0 + j + 1, 0:H],
                                     in_=strip[:]).then_inc(d_bc, 16)
                    g_.wait_ge(d_bc, 160 * i + 64 * hf + 32)
                    for j, dst in enumerate((mub, rb)):
                        row = bcdD_ap[r0 + j: r0 + j + 1, 0:H]
                        g_.dma_start(out=dst[:, tsl(hf, H)],
                                     in_=bass.AP(row.tensor, row.offset, [[0, P], [1, H]])
                                     ).then_inc(d_bc, 16)
                g_.wait_ge(d_evx, 4 * (i + 1))
                base = 18 * (i % 2)
                g_.dma_start(out=bcdD_ap[base:base + 16, 0:Q], in_=Bsb[:]).then_inc(d_bc, 16)
                g_.dma_start(out=bcdD_ap[72 + base:72 + base + 16, 0:Q],
                             in_=Csb[:]).then_inc(d_bc, 16)
                g_.wait_ge(d_bc, 160 * (i + 1))
                for n in range(17):
                    if n < 16:
                        w = 128 * i + 8 * (n - 1)
                        if n < 2:
                            w = 128 * i
                        if w > 0:
                            g_.wait_ge(d_scn, w)
                        rowB = bcdD_ap[base + n:base + n + 1, 0:Q]
                        rowC = bcdD_ap[72 + base + n:72 + base + n + 1, 0:Q]
                        g_.dma_start(out=Brep[n % 2][:],
                                     in_=bass.AP(rowB.tensor, rowB.offset, [[0, P], [1, Q]])
                                     ).then_inc(d_bcn, 16)
                        g_.dma_start(out=Crep[n % 2][:],
                                     in_=bass.AP(rowC.tensor, rowC.offset, [[0, P], [1, Q]])
                                     ).then_inc(d_bcn, 16)
                    if n >= 1:
                        nm = n - 1
                        g_.wait_ge(d_bcn, 512 * i + 32 * (nm + 1))
                        for dt in range(CT):
                            idx = 128 * i + 8 * nm + dt
                            g_.wait_ge(d_dg, 8 * i + dt + 1)
                            if idx >= 2:
                                g_.wait_ge(d_scn, idx - 1)
                            g_.tensor_mul(ut[idx % 2][:], g[:, tsl(dt, Q)],
                                          Brep[nm % 2][:]).then_inc(d_ut, 1)

        @blk.scalar
        def _(s):
            for i in range(NCH):
                s.wait_ge(d_in, 16 * (i + 1))
                for hf in range(2):
                    for ct in range(CT):
                        idx = 16 * i + 8 * hf + ct
                        if idx >= 2:
                            s.wait_ge(d_m2, idx - 1)
                        s.activation(sq[ct % 2][:], xin[:, ct * Q + hf * H: ct * Q + (hf + 1) * H],
                                     AF.Square).then_inc(d_sq, 1)
                for hf in range(2):
                    s.wait_ge(d_sd, 2 * i + hf + 1)
                    kb.chain("s", s, s.activation(m2s[:], vars_[:], AF.Ln, bias=kb.ep1[:]))
                    s.activation(rstd[:], m2s[:], AF.Exp, scale=kb.mh1[:]).then_inc(d_sa, 1)
                for hf in range(2):
                    s.wait_ge(d_mmx, 2 * i + hf + 1)
                    s.activation(dpre_bf[:, tsl(hf, H)], ps_dp[:], AF.Copy
                                 ).then_inc(d_evx, 1)
                    for dt in range(CT):
                        idx = 16 * i + 8 * hf + dt
                        s.wait_ge(d_mmd, idx + 1)
                        kb.chain("s", s, s.activation(Ed[dt % 2][:], ps_dr[dt % 2][:], AF.Exp,
                                                      bias=cv[:, CV_BDT + dt:CV_BDT + dt + 1]))
                        s.activation(delta[:, dt * Q + hf * H: dt * Q + (hf + 1) * H],
                                     Ed[dt % 2][:], AF.Ln, bias=kb.one_pp[:]).then_inc(d_evd, 1)
                for n in range(16):
                    for dt in range(CT):
                        idx = 128 * i + 8 * n + dt
                        if idx >= 2:
                            s.wait_ge(d_scn, idx - 1)
                        s.activation(at[idx % 2][:], delta[:, tsl(dt, Q)], AF.Exp,
                                     scale=dv[:, DV_ANEG + dt * 16 + n:DV_ANEG + dt * 16 + n + 1]
                                     ).then_inc(d_aex, 1)

        @blk.tensor
        def _(t):
            for i in range(NCH):
                t.wait_ge(d_in, 16 * (i + 1))
                if i >= 1:
                    t.wait_ge(d_sev, 2 * i)
                for hf in range(2):
                    if hf == 1:
                        t.wait_ge(d_sev, 2 * i + 1)
                    for ct in range(CT):
                        t.matmul(ps_m1[:], kb.c1024[:], xin[:, ct * Q + hf * H: ct * Q + (hf + 1) * H],
                                 start=(ct == 0), stop=(ct == CT - 1))
                    for ct in range(CT):
                        t.wait_ge(d_sq, 16 * i + 8 * hf + ct + 1)
                        t.matmul(ps_m2[:], kb.c1024[:], sq[ct % 2][:],
                                 start=(ct == 0), stop=(ct == CT - 1))
                        t.matmul(ps_junk[:], jb[:, 0:1], jb[:, 0:8],
                                 start=True, stop=True).then_inc(d_m2, 1)
                t.wait_ge(d_w, 32)
                t.wait_ge(d_ap, 8 * (i + 1))
                for hf in range(2):
                    if 2 * i + hf >= 1:
                        t.wait_ge(d_evx, 2 * (2 * i + hf))
                    for ct in range(CT):
                        t.wait_ge(d_gb, 16 * i + 8 * hf + ct + 1)
                        gb_ = gbt[ct % 2]
                        t.matmul(ps_dp[:], wxpb[:, ct * 96: ct * 96 + 64], gb_[:],
                                 start=(ct == 0), stop=(ct == CT - 1))
                        t.matmul(ps_B[:], wxpb[:, ct * 96 + 64: ct * 96 + 80], gb_[:],
                                 start=(ct == 0), stop=(ct == CT - 1))
                        t.matmul(ps_C[:], wxpb[:, ct * 96 + 80: ct * 96 + 96], gb_[:],
                                 start=(ct == 0), stop=(ct == CT - 1))
                        t.matmul(ps_junk[:], jb[:, 0:1], jb[:, 0:8],
                                 start=True, stop=True).then_inc(d_mmg, 1)
                    t.matmul(ps_junk[:], jb[:, 0:1], jb[:, 0:8],
                             start=True, stop=True).then_inc(d_mmx, 1)
                    t.wait_ge(d_evx, 2 * (2 * i + hf) + 1)
                    for dt in range(CT):
                        idx = 16 * i + 8 * hf + dt
                        if idx >= 2:
                            t.wait_ge(d_evd, idx - 1)
                        t.matmul(ps_dr[dt % 2][:], wdtb[:, tsl(dt, P)], dpre_bf[:, tsl(hf, H)],
                                 start=True, stop=True)
                        t.matmul(ps_junk[:], jb[:, 0:1], jb[:, 0:8],
                                 start=True, stop=True).then_inc(d_mmd, 1)

        @blk.vector
        def _(v):
            CHV = lambda ins: kb.chain("v", v, ins)
            v.memset(hstate[:], 0.0)
            for i in range(NCH):
                for hf in range(2):
                    v.wait_ge(d_m2, 16 * i + 8 * hf + 8)
                    if hf == 1:
                        v.wait_ge(d_bc, 160 * i + 32)
                    CHV(v.tensor_copy(mus[:], ps_m1[:]))
                    v.tensor_copy(m2s[:], ps_m2[:]).then_inc(d_sev, 1)
                    CHV(v.tensor_mul(vars_[:], mus[:], mus[:]))
                    v.tensor_sub(vars_[:], m2s[:], vars_[:]).then_inc(d_sd, 1)
                    v.wait_ge(d_sa, 2 * i + hf + 1)
                    v.tensor_mul(mus[:], mus[:], rstd[:]).then_inc(d_sd2, 1)
                v.wait_ge(d_bc, 160 * i + 128)
                for ct in range(CT):
                    CHV(v.tensor_mul(sq[0][:], xin[:, ct * Q: ct * Q + H], rb[:, 0:H]))
                    CHV(v.tensor_mul(sq[1][:], xin[:, ct * Q + H: (ct + 1) * Q], rb[:, H:Q]))
                    CHV(v.tensor_sub(g[:, ct * Q: ct * Q + H], sq[0][:], mub[:, 0:H]))
                    v.tensor_sub(g[:, ct * Q + H: (ct + 1) * Q], sq[1][:], mub[:, H:Q]
                                 ).then_inc(d_ap, 1)
                for hf in range(2):
                    for ct in range(CT):
                        idx = 16 * i + 8 * hf + ct
                        if idx >= 2:
                            v.wait_ge(d_mmg, idx - 1)
                        v.tensor_copy(gbt[ct % 2][:], g[:, ct * Q + hf * H: ct * Q + (hf + 1) * H]
                                      ).then_inc(d_gb, 1)
                    v.wait_ge(d_mmx, 2 * i + hf + 1)
                    if hf == 0 and i > 0:
                        v.wait_ge(d_bc, 160 * i)
                    CHV(v.tensor_copy(Bsb[:, tsl(hf, H)], ps_B[:]))
                    v.tensor_copy(Csb[:, tsl(hf, H)], ps_C[:]).then_inc(d_evx, 1)
                v.wait_ge(d_evd, 16 * (i + 1))
                for dt in range(CT):
                    v.tensor_scalar_mul(xin[:, tsl(dt, Q)], g[:, tsl(dt, Q)],
                                        cv[:, CV_DP + dt:CV_DP + dt + 1])
                for ct in range(CT):
                    v.tensor_mul(g[:, tsl(ct, Q)], delta[:, tsl(ct, Q)], g[:, tsl(ct, Q)]
                                 ).then_inc(d_dg, 1)
                for n in range(16):
                    v.wait_ge(d_bcn, 512 * i + 32 * (n + 1))
                    for dt in range(CT):
                        idx = 128 * i + 8 * n + dt
                        v.wait_ge(d_aex, idx + 1)
                        v.wait_ge(d_ut, idx + 1)
                        par = idx % 2
                        col = hstate[:, dt * 16 + n: dt * 16 + n + 1]
                        CHV(v.tensor_tensor_scan(ht[par][:], at[par][:], ut[par][:],
                                                 col, ALU.mult, ALU.add))
                        CHV(v.tensor_tensor(col, ht[par][:, Q - 1:Q],
                                            ht[par][:, Q - 1:Q], ALU.max))
                        if n == 15:
                            v.tensor_mul(yt[:], ht[par][:], Crep[n % 2][:]
                                         ).then_inc(d_scn, 1)
                            v.tensor_add(xin[:, tsl(dt, Q)], xin[:, tsl(dt, Q)],
                                         yt[:]).then_inc(d_fin, 1)
                        else:
                            v.tensor_mul(yt[:], ht[par][:], Crep[n % 2][:])
                            v.tensor_add(xin[:, tsl(dt, Q)], xin[:, tsl(dt, Q)],
                                         yt[:]).then_inc(d_scn, 1)
    return es
def _r8(v):
    return np.ascontiguousarray(np.asarray(v, np.float32).reshape(8, 128).T)

def _r32(v):
    return np.ascontiguousarray(np.asarray(v, np.float32).reshape(32, 128).T)

def _pack_cvec(d):
    cvp = np.zeros((128, NCV), np.float32)
    cvp[:, 0:8] = _r8(d["ts"].reshape(-1))
    cvp[:, 8:16] = _r8(d["u"])
    cvp[:, 16:24] = _r8(d["ln1w"])
    cvp[:, 24:32] = _r8(d["ln1b"])
    cvp[:, 32:40] = _r8(d["ln2w"])
    cvp[:, 40:48] = _r8(d["ln2b"])
    cvp[:, 48:56] = _r8(d["bdt"])
    cvp[:, 56:64] = _r8(d["Dp"])
    cvp[:, 64:72] = _r8(d["b2"])
    cvp[:, 72:104] = _r32(d["b1"])
    cvp[:, 104:136] = _r32(d["th"])
    cvp[:, 136:168] = _r32(d["qwr"])
    cvp[:, 168:200] = _r32(d["qwi"])
    al = np.asarray(d["alog"], np.float32)
    cvp[:, 200:328] = al.reshape(8, 128, 16).transpose(1, 0, 2).reshape(128, 128)
    return cvp

def _tr(a):
    return np.ascontiguousarray(np.asarray(a, np.float32).T)

def _build():
    nc = bass.Bass("TRN2", target_bir_lowering=False, debug=False)
    F32_ = F32
    ins = {}
    def I(name, shape):
        ins[name] = nc.dram_tensor(name, shape, F32_, kind="ExternalInput").ap()
        return ins[name]
    xT = I("xT", [C, T])
    cvec = I("cvec", [P, NCV])
    wkT = I("wkT", [1024, 1024]); wvT = I("wvT", [1024, 1024]); wrT = I("wrT", [1024, 1024])
    wckT = I("wckT", [1024, 4096]); wcvT = I("wcvT", [4096, 1024]); wcrT = I("wcrT", [1024, 1024])
    w1n = I("w1n", [1024, 4096]); w2n = I("w2n", [4096, 1024])
    wxpT = I("wxpT", [1024, 96]); wdtT = I("wdtT", [64, 1024])
    wfcT = I("wfcT", [1024, 4096]); wprojT = I("wprojT", [4096, 1024])
    outT = nc.dram_tensor("outT", [C, T], F32_, kind="ExternalOutput").ap()
    a_d = nc.dram_tensor("a_d", [C, T], F32_, kind="Internal").ap()
    x1_d = nc.dram_tensor("x1_d", [C, T], F32_, kind="Internal").ap()
    x2_d = nc.dram_tensor("x2_d", [C, T], F32_, kind="Internal").ap()
    dp_d = nc.dram_tensor("dp_d", [C, T], F32_, kind="Internal").ap()
    bcd = nc.dram_tensor("bcd", [8, 1024], F32_, kind="Internal").ap()
    bcdD = nc.dram_tensor("bcdD", [110, 1024], F32_, kind="Internal").ap()

    kb = KB(nc)
    kb.alloc_consts()
    kb.emit_consts(cvec)
    kb._ces.close()
    emit_phase_A(kb, xT, a_d, wkT, wvT, wrT, bcd).close()
    emit_phase_B(kb, xT, a_d, x1_d, wckT, wcvT, wcrT, bcd).close()
    emit_phase_C(kb, x1_d, x2_d, w1n, w2n, bcd).close()
    emit_phase_D(kb, x2_d, dp_d, wxpT, wdtT, bcdD).close()
    emit_phase_E(kb, x2_d, dp_d, outT, wfcT, wprojT, bcd).close()
    return nc


def kernel(L0_attn_x, L0_attn_time_decay, L0_attn_time_first, L0_attn_time_shift,
           L0_attn_Wk, L0_attn_Wv, L0_attn_Wr, L0_attn_Wck, L0_attn_Wcv, L0_attn_Wcr,
           L0_attn_ln1_w, L0_attn_ln1_b, L0_attn_ln2_w, L0_attn_ln2_b,
           L0_ffn_W1, L0_ffn_b1, L0_ffn_quantum_phases, L0_ffn_quantum_weights_real,
           L0_ffn_quantum_weights_imag, L0_ffn_W2, L0_ffn_b2,
           L1_attn_W_xproj, L1_attn_W_dt, L1_attn_b_dt, L1_attn_A_log, L1_attn_D_param,
           L1_ffn_W_fc, L1_ffn_W_proj):
    cvp = _pack_cvec({
        "ts": np.asarray(L0_attn_time_shift), "u": L0_attn_time_first,
        "ln1w": L0_attn_ln1_w, "ln1b": L0_attn_ln1_b,
        "ln2w": L0_attn_ln2_w, "ln2b": L0_attn_ln2_b,
        "bdt": L1_attn_b_dt, "Dp": L1_attn_D_param, "b2": L0_ffn_b2,
        "b1": L0_ffn_b1, "th": L0_ffn_quantum_phases,
        "qwr": L0_ffn_quantum_weights_real, "qwi": L0_ffn_quantum_weights_imag,
        "alog": L1_attn_A_log,
    })
    shared = {
        "cvec": cvp,
        "wkT": _tr(L0_attn_Wk), "wvT": _tr(L0_attn_Wv), "wrT": _tr(L0_attn_Wr),
        "wckT": _tr(L0_attn_Wck), "wcvT": _tr(L0_attn_Wcv), "wcrT": _tr(L0_attn_Wcr),
        "w1n": np.ascontiguousarray(np.asarray(L0_ffn_W1, np.float32)),
        "w2n": np.ascontiguousarray(np.asarray(L0_ffn_W2, np.float32)),
        "wxpT": _tr(L1_attn_W_xproj), "wdtT": _tr(L1_attn_W_dt),
        "wfcT": _tr(L1_ffn_W_fc), "wprojT": _tr(L1_ffn_W_proj),
    }
    x = np.asarray(L0_attn_x, np.float32)
    nc = _build()
    in_maps = [dict(shared, xT=_tr(x[b])) for b in range(8)]
    res = run_bass_kernel_spmd(nc, in_maps, core_ids=list(range(8)))
    out = np.stack([np.ascontiguousarray(res.results[b]["outT"].T) for b in range(8)], axis=0)
    return out.astype(np.float32)


# version 06 (final)
# speedup vs baseline: 1.1051x; 1.1051x over previous
import contextlib
import numpy as np
import concourse.bass as bass
import concourse.mybir as mybir
from concourse.bass_utils import run_bass_kernel_spmd

import contextlib
import numpy as np
import concourse.bass as bass
import concourse.mybir as mybir

F32 = mybir.dt.float32
BF16 = mybir.dt.bfloat16
AF = mybir.ActivationFunctionType
ALU = mybir.AluOpType

P = 128
C = 1024
CT = 8
T = 8192
F = 4096
FT = 32
EPS = 1e-5
PIH = float(np.pi / 2)

CV_TS, CV_U, CV_LN1W, CV_LN1B, CV_LN2W, CV_LN2B, CV_BDT, CV_DP, CV_B2 = \
    0, 8, 16, 24, 32, 40, 48, 56, 64
CV_B1, CV_TH, CV_QWR, CV_QWI, CV_ALOG = 72, 104, 136, 168, 200
NCV = 328

DV_TS1, DV_NEGTS, DV_ONEMTS, DV_U = 0, 8, 16, 24
DV_STH, DV_CTH, DV_CQP, DV_ANEG = 32, 64, 96, 128
DV_B1S, DV_B1SP = 256, 288
NDV = 320


def tsl(i, n):
    return slice(i * n, (i + 1) * n)


class KB:
    def __init__(self, nc):
        self.nc = nc
        self.sem = {}
        self.selfn = {}

    def S(self, name):
        if name not in self.sem:
            self.sem[name] = self.nc.alloc_semaphore(name)
        return self.sem[name]

    def chain(self, tag, eng, ins):
        sem = self.S("self_" + tag)
        n = self.selfn.get(tag, 0) + 1
        self.selfn[tag] = n
        ins.then_inc(sem, 1)
        eng.wait_ge(sem, n)
        return ins

    def alloc_consts(self):
        nc = self.nc
        self.cv = nc.alloc_sbuf_tensor("cv", [P, NCV], F32)
        self.dv = nc.alloc_sbuf_tensor("dv", [P, NDV], F32)
        self.c1024 = nc.alloc_sbuf_tensor("c1024", [P, 1], F32)
        self.ep1 = nc.alloc_sbuf_tensor("ep1", [1, 1], F32)
        self.mh1 = nc.alloc_sbuf_tensor("mh1", [1, 1], F32)
        self.one1 = nc.alloc_sbuf_tensor("one1", [1, 1], F32)
        self.one_pp = nc.alloc_sbuf_tensor("one_pp", [P, 1], F32)
        self.c01_pp = nc.alloc_sbuf_tensor("c01_pp", [P, 1], F32)
        self.jbg = nc.alloc_sbuf_tensor("jbg", [P, 16], BF16)

    def emit_consts(self, cvec_ap):
        kbself = self
        nc = self.nc
        ces = contextlib.ExitStack()
        self.half_pp = ces.enter_context(nc.sbuf_tensor("half_pp", [P, 1], F32))
        self.sc32a = ces.enter_context(nc.sbuf_tensor("sc32a", [P, 32], F32))
        self.sc32b = ces.enter_context(nc.sbuf_tensor("sc32b", [P, 32], F32))
        self.sc128 = ces.enter_context(nc.sbuf_tensor("sc128", [P, 128], F32))
        self._ces = ces
        cv, dv = self.cv, self.dv
        ld, sv, ss = self.S("cn_ld"), self.S("cn_v"), self.S("cn_s")
        with nc.Block() as blk:
            @blk.sync
            def _(sync):
                sync.dma_start(out=cv[:], in_=cvec_ap).then_inc(ld, 16)

            @blk.vector
            def _(v):
                CH = lambda ins: kbself.chain("v", v, ins)
                v.memset(self.c1024[:], 1.0 / 1024.0)
                v.memset(self.jbg[:], 1.0)
                v.memset(self.ep1[:], EPS)
                v.memset(self.mh1[:], -0.5)
                v.memset(self.one1[:], 1.0)
                v.memset(self.one_pp[:], 1.0)
                v.memset(self.half_pp[:], 0.5)
                v.memset(self.c01_pp[:], 0.1)
                v.wait_ge(ld, 16)
                tsv = cv[:, CV_TS:CV_TS + 8]
                CH(v.tensor_scalar_add(dv[:, DV_TS1:DV_TS1 + 8], tsv, 1.0))
                CH(v.tensor_scalar_mul(dv[:, DV_NEGTS:DV_NEGTS + 8], tsv, -1.0))
                CH(v.tensor_scalar(dv[:, DV_ONEMTS:DV_ONEMTS + 8], tsv, -1.0, 1.0,
                                   ALU.mult, ALU.add))
                th = cv[:, CV_TH:CV_TH + 32]
                pa = self.sc32a
                pb = self.sc128
                msk = self.sc32b
                CH(v.tensor_copy(pa[:], th))
                for _ in range(6):
                    CH(v.tensor_single_scalar(msk[:], pa[:], np.pi, ALU.is_gt))
                    CH(v.scalar_tensor_tensor(pb[:, 0:32], msk[:], -2.0 * np.pi,
                                              pa[:], ALU.mult, ALU.add))
                    CH(v.tensor_single_scalar(msk[:], pb[:, 0:32], -np.pi, ALU.is_lt))
                    CH(v.scalar_tensor_tensor(pa[:], msk[:], 2.0 * np.pi,
                                              pb[:, 0:32], ALU.mult, ALU.add))
                v.tensor_copy(self.sc32b[:], pa[:]).then_inc(sv, 1)

            @blk.scalar
            def _(s):
                CHS = lambda ins: kbself.chain("s", s, ins)
                s.wait_ge(sv, 1)
                CHS(s.activation(dv[:, DV_STH:DV_STH + 32], self.sc32a[:], AF.Sin))
                CHS(s.activation(self.sc32b[:], self.sc32a[:], AF.Sin, scale=self.half_pp[:]))
                CHS(s.activation(dv[:, DV_U:DV_U + 8], cv[:, CV_U:CV_U + 8], AF.Exp))
                s.activation(self.sc128[:], cv[:, CV_ALOG:CV_ALOG + 128],
                             AF.Exp).then_inc(ss, 1)

            @blk.vector
            def _(v):
                CH2 = lambda ins: kbself.chain("v", v, ins)
                v.wait_ge(ss, 1)
                CH2(v.tensor_scalar_mul(dv[:, DV_ANEG:DV_ANEG + 128], self.sc128[:], -1.0))
                CH2(v.tensor_mul(self.sc32a[:], self.sc32b[:], self.sc32b[:]))
                CH2(v.tensor_scalar(dv[:, DV_CTH:DV_CTH + 32], self.sc32a[:], -2.0, 1.0,
                                    ALU.mult, ALU.add))
                CH2(v.tensor_mul(self.sc32a[:], dv[:, DV_CTH:DV_CTH + 32],
                                 cv[:, CV_QWR:CV_QWR + 32]))
                CH2(v.tensor_mul(self.sc32b[:], dv[:, DV_STH:DV_STH + 32],
                                 cv[:, CV_QWI:CV_QWI + 32]))
                CH2(v.tensor_add(self.sc128[:, 0:32], self.sc32a[:], self.sc32b[:]))
                CH2(v.tensor_mul(self.sc32a[:], self.sc128[:, 0:32], cv[:, CV_QWI:CV_QWI + 32]))
                CH2(v.tensor_scalar_mul(dv[:, DV_CQP:DV_CQP + 32], self.sc32a[:], 0.1))
                CH2(v.tensor_scalar_mul(dv[:, DV_B1S:DV_B1S + 32], cv[:, CV_B1:CV_B1 + 32], 0.1))
                CH2(v.tensor_scalar(dv[:, DV_B1SP:DV_B1SP + 32], cv[:, CV_B1:CV_B1 + 32],
                                    0.1, PIH, ALU.mult, ALU.add))


def emit_phase_C(kb, x1_ap, x2_ap, w1_ap, w2_ap, bcd_ap, NCH=16, dbg=None):
    nc = kb.nc
    Q = 512
    S = kb.S
    c_w, c_in, c_sq, c_m2, c_sd, c_sa, c_sd2, c_bc = (
        S("c_w"), S("c_in"), S("c_sq"), S("c_m2"), S("c_sd"), S("c_sa"),
        S("c_sd2"), S("c_bc"))
    c_t0, c_gb, c_mmh, c_ev, c_q, c_ab, c_mmo, c_fin, c_st = (
        S("c_t0"), S("c_gb"), S("c_mmh"), S("c_ev"), S("c_q"), S("c_ab"),
        S("c_mmo"), S("c_fin"), S("c_st"))

    es = contextlib.ExitStack()
    A = lambda name, shape, dt=F32: es.enter_context(nc.sbuf_tensor(name, shape, dt))
    PSUM = lambda name, shape, dt=F32: es.enter_context(nc.psum_tensor(name, shape, dt))
    w1b = A("c_w1b", [P, CT * F], BF16)
    w2b = A("c_w2b", [P, FT * C], BF16)
    xin = A("c_xin", [P, CT * Q], F32)
    gbf = A("c_gbf", [P, CT * Q], BF16)
    abf = A("c_abf", [P, FT * Q], BF16)
    sq = [A(f"c_sq{j}", [P, Q], F32) for j in range(2)]
    tmp = sq
    s1t = [A(f"c_s1t{j}", [P, Q], BF16) for j in range(2)]
    c1t = [A(f"c_c1t{j}", [P, Q], BF16) for j in range(2)]
    mus = A("c_mus", [1, Q], F32)
    m2s = A("c_m2s", [1, Q], F32)
    vars_ = A("c_vars", [1, Q], F32)
    rstd = A("c_rstd", [1, Q], F32)
    mub = A("c_mub", [P, Q], F32)
    rb = A("c_rb", [P, Q], F32)
    ps_m1 = PSUM("c_psm1", [1, Q], F32)
    ps_m2 = PSUM("c_psm2", [1, Q], F32)
    ps_h = [PSUM(f"c_psh{j}", [P, Q], F32) for j in range(2)]
    ps_o = [PSUM(f"c_pso{j}", [P, Q], F32) for j in range(2)]
    jb = kb.jbg
    ps_junk = PSUM("c_psj", [1, 8], F32)

    x1r = x1_ap.rearrange("(ct p) t -> p ct t", p=P)
    x2r = x2_ap.rearrange("(ct p) t -> p ct t", p=P)
    w1r = w1_ap.rearrange("(dt p) f -> p dt f", p=P)
    w2r = w2_ap.rearrange("(ft p) d -> p ft d", p=P)

    cv, dv = kb.cv, kb.dv

    with nc.Block() as blk:
        @blk.sync
        def _(sync):
            for i in range(NCH):
                if i > 0:
                    sync.wait_ge(c_fin, 8 * i)
                    sync.dma_start(out=x2r[:, :, tsl(i - 1, Q)],
                                   in_=xin.ap().rearrange("p (ct q) -> p ct q", ct=CT)
                                   ).then_inc(c_st, 16)
                if i > 0:
                    sync.wait_ge(c_st, 16 * i)
                sync.dma_start(out=xin.ap().rearrange("p (ct q) -> p ct q", ct=CT),
                               in_=x1r[:, :, tsl(i, Q)]).then_inc(c_in, 16)
            sync.wait_ge(c_fin, 8 * NCH)
            sync.dma_start(out=x2r[:, :, tsl(NCH - 1, Q)],
                           in_=xin.ap().rearrange("p (ct q) -> p ct q", ct=CT)
                           ).then_inc(c_st, 16)
            sync.wait_ge(c_st, 16 * NCH)
            if dbg is not None:
                nd = 0
                sync.wait_ge(S("c_dbg"), 1)
                for name, src in (("gbf", gbf), ("abf", abf), ("mub", mub),
                                  ("rb", rb), ("s1", s1t[1]), ("c1", c1t[1]), ("w1b", w1b),
                                  ("hid", sq[0]), ("hid0", sq[1]), ("dv", kb.dv), ("cv", kb.cv)):
                    if name in dbg:
                        sync.dma_start(out=dbg[name], in_=src[:]).then_inc(c_st, 16)
                        nd += 1
                sync.wait_ge(c_st, 16 * (NCH + nd))

        @blk.gpsimd
        def _(g):
            g.dma_start(out=w1b.ap().rearrange("p (dt f) -> p dt f", dt=CT),
                        in_=w1r).then_inc(c_w, 16)
            g.dma_start(out=w2b.ap().rearrange("p (ft d) -> p ft d", ft=FT),
                        in_=w2r).then_inc(c_w, 16)
            for i in range(NCH):
                g.wait_ge(c_sd2, i + 1)
                for j, strip in enumerate((mus, rstd)):
                    g.dma_start(out=bcd_ap[2 * (i % 2) + j: 2 * (i % 2) + j + 1, 0:Q],
                                in_=strip[:]).then_inc(c_bc, 16)
                g.wait_ge(c_bc, 64 * i + 32)
                for j, dst in enumerate((mub, rb)):
                    row = bcd_ap[2 * (i % 2) + j: 2 * (i % 2) + j + 1, 0:Q]
                    g.dma_start(out=dst[:], in_=bass.AP(row.tensor, row.offset,
                                                        [[0, P], [1, Q]])).then_inc(c_bc, 16)

        @blk.scalar
        def _(s):
            for i in range(NCH):
                for ct in range(CT):
                    s.wait_ge(c_in, 16 * (i + 1))
                    if i * 8 + ct >= 2:
                        s.wait_ge(c_m2, 8 * i + ct - 1)
                    s.activation(sq[ct % 2][:], xin[:, tsl(ct, Q)], AF.Square
                                 ).then_inc(c_sq, 1)
                s.wait_ge(c_sd, i + 1)
                kb.chain("s", s, s.activation(m2s[:], vars_[:], AF.Ln, bias=kb.ep1[:]))
                s.activation(rstd[:], m2s[:], AF.Exp, scale=kb.mh1[:]).then_inc(c_sa, 1)
                for ft in range(FT):
                    s.wait_ge(c_mmh, FT * i + ft + 1)
                    if FT * i + ft >= 2:
                        s.wait_ge(c_q, FT * i + ft - 1)
                    s.activation(s1t[ft % 2][:], ps_h[ft % 2][:], AF.Sin,
                                 scale=kb.c01_pp[:], bias=dv[:, DV_B1S + ft:DV_B1S + ft + 1])
                    s.activation(c1t[ft % 2][:], ps_h[ft % 2][:], AF.Sin,
                                 scale=kb.c01_pp[:], bias=dv[:, DV_B1SP + ft:DV_B1SP + ft + 1]
                                 ).then_inc(c_ev, 2)
                    if ft >= 1:
                        s.wait_ge(c_q, FT * i + ft)
                        s.activation(abf[:, tsl(ft - 1, Q)], c1t[(ft - 1) % 2][:],
                                     AF.Relu, bias=cv[:, CV_B1 + ft - 1:CV_B1 + ft]
                                     ).then_inc(c_ab, 1)
                s.wait_ge(c_q, FT * (i + 1))
                s.activation(abf[:, tsl(FT - 1, Q)], c1t[(FT - 1) % 2][:],
                             AF.Relu, bias=cv[:, CV_B1 + FT - 1:CV_B1 + FT]
                             ).then_inc(c_ab, 1)
            if dbg is not None:
                s.wait_ge(c_fin, 8 * NCH)
                s.activation(sq[0][:], ps_h[1][:], AF.Copy)
                s.activation(sq[1][:], ps_h[0][:], AF.Copy).then_inc(S("c_dbg"), 1)

        @blk.tensor
        def _(t):
            t.wait_ge(S("c_ji"), 1)
            for i in range(NCH):
                t.wait_ge(c_in, 16 * (i + 1))
                if i >= 1:
                    t.wait_ge(c_sd, i)
                for ct in range(CT):
                    t.matmul(ps_m1[:], kb.c1024[:], xin[:, tsl(ct, Q)],
                             start=(ct == 0), stop=(ct == CT - 1))
                for ct in range(CT):
                    t.wait_ge(c_sq, 8 * i + ct + 1)
                    t.matmul(ps_m2[:], kb.c1024[:], sq[ct % 2][:],
                             start=(ct == 0), stop=(ct == CT - 1))
                    t.matmul(ps_junk[:], jb[:, 0:1], jb[:, 0:8],
                             start=True, stop=True).then_inc(c_m2, 1)
                t.wait_ge(c_w, 32)
                t.wait_ge(c_gb, 8 * (i + 1))
                for ft in range(FT):
                    if FT * i + ft >= 2:
                        t.wait_ge(c_q, FT * i + ft - 1)
                    for ct in range(CT):
                        t.matmul(ps_h[ft % 2][:], w1b[:, ct * F + ft * P: ct * F + (ft + 1) * P],
                                  gbf[:, tsl(ct, Q)], start=(ct == 0), stop=(ct == CT - 1))
                    t.matmul(ps_junk[:], jb[:, 0:1], jb[:, 0:8],
                             start=True, stop=True).then_inc(c_mmh, 1)
                t.wait_ge(c_ab, FT * (i + 1))
                for dt in range(CT):
                    if 8 * i + dt >= 2:
                        t.wait_ge(c_fin, 8 * i + dt - 1)
                    for ft in range(FT):
                        t.matmul(ps_o[dt % 2][:], w2b[:, ft * C + dt * P: ft * C + (dt + 1) * P],
                                  abf[:, tsl(ft, Q)], start=(ft == 0), stop=(ft == FT - 1))
                    t.matmul(ps_junk[:], jb[:, 0:1], jb[:, 0:8],
                             start=True, stop=True).then_inc(c_mmo, 1)

        @blk.vector
        def _(v):
            v.memset(jb[:], 1.0).then_inc(S("c_ji"), 1)
            for i in range(NCH):
                v.wait_ge(c_m2, 8 * (i + 1))
                CHV = lambda ins: kb.chain("v", v, ins)
                CHV(v.tensor_copy(mus[:], ps_m1[:]))
                CHV(v.tensor_copy(m2s[:], ps_m2[:]))
                CHV(v.tensor_mul(vars_[:], mus[:], mus[:]))
                v.tensor_sub(vars_[:], m2s[:], vars_[:]).then_inc(c_sd, 1)
                v.wait_ge(c_sa, i + 1)
                v.tensor_mul(mus[:], mus[:], rstd[:]).then_inc(c_sd2, 1)
                v.wait_ge(c_bc, 64 * (i + 1))
                for ct in range(CT):
                    v.tensor_mul(tmp[ct % 2][:], xin[:, tsl(ct, Q)], rb[:])
                    v.tensor_sub(gbf[:, tsl(ct, Q)], tmp[ct % 2][:], mub[:]
                                 ).then_inc(c_gb, 1)
                for ft in range(FT):
                    v.wait_ge(c_ev, 64 * i + 2 * ft + 2)
                    if FT * i + ft >= 2:
                        v.wait_ge(c_ab, FT * i + ft - 1)
                    v.tensor_scalar_mul(abf[:, tsl(ft, Q)], s1t[ft % 2][:],
                                        dv[:, DV_CTH + ft:DV_CTH + ft + 1])
                    v.scalar_tensor_tensor(s1t[ft % 2][:], c1t[ft % 2][:],
                                           dv[:, DV_STH + ft:DV_STH + ft + 1],
                                           abf[:, tsl(ft, Q)], ALU.mult, ALU.add)
                    v.scalar_tensor_tensor(c1t[ft % 2][:], s1t[ft % 2][:],
                                           dv[:, DV_CQP + ft:DV_CQP + ft + 1],
                                           ps_h[ft % 2][:], ALU.mult, ALU.add
                                           ).then_inc(c_q, 1)
                for dt in range(CT):
                    v.wait_ge(c_mmo, 8 * i + dt + 1)
                    v.scalar_tensor_tensor(xin[:, tsl(dt, Q)], ps_o[dt % 2][:],
                                           cv[:, CV_B2 + dt:CV_B2 + dt + 1],
                                           xin[:, tsl(dt, Q)], ALU.add, ALU.add
                                           ).then_inc(c_fin, 1)


    return es

def emit_phase_E(kb, x2_ap, dpart_ap, x4_ap, wfc_ap, wproj_ap, bcd_ap, NCH=32):
    nc = kb.nc
    Q = 256
    S = kb.S
    e_w, e_in, e_sq, e_m2, e_sd, e_sa, e_sd2, e_bc = (
        S("e_w"), S("e_in"), S("e_sq"), S("e_m2"), S("e_sd"), S("e_sa"),
        S("e_sd2"), S("e_bc"))
    e_t0, e_gb, e_mmh, e_ev, e_q2, e_mmo, e_spk, e_fin, e_st = (
        S("e_t0"), S("e_gb"), S("e_mmh"), S("e_ev"), S("e_q2"), S("e_mmo"),
        S("e_spk"), S("e_fin"), S("e_st"))

    es = contextlib.ExitStack()
    A = lambda name, shape, dt=F32: es.enter_context(nc.sbuf_tensor(name, shape, dt))
    PSUM = lambda name, shape, dt=F32: es.enter_context(nc.psum_tensor(name, shape, dt))
    wfb = A("e_wfb", [P, CT * F], BF16)
    wpb = A("e_wpb", [P, FT * C], BF16)
    xin = [A(f"e_xin{j}", [P, CT * Q], F32) for j in range(2)]
    gbf = A("e_gbf", [P, CT * Q], BF16)
    hbf = A("e_hbf", [P, FT * Q], BF16)
    ysb = [A(f"e_ysb{j}", [P, CT * Q], F32) for j in range(2)]
    acc = A("e_acc", [P, CT * Q], F32)
    sq = [A(f"e_sq{j}", [P, Q], F32) for j in range(2)]
    st_ = [A(f"e_st{j}", [P, Q], F32) for j in range(2)]
    mem = [A(f"e_mem{j}", [P, Q], F32) for j in range(2)]
    spk = A("e_spk", [P, Q], F32)
    mus = A("e_mus", [1, Q], F32)
    m2s = A("e_m2s", [1, Q], F32)
    vars_ = A("e_vars", [1, Q], F32)
    rstd = A("e_rstd", [1, Q], F32)
    mub = A("e_mub", [P, Q], F32)
    rb = A("e_rb", [P, Q], F32)
    ps_m1 = PSUM("e_psm1", [1, Q], F32)
    ps_m2 = PSUM("e_psm2", [1, Q], F32)
    ps_h = [PSUM(f"e_psh{j}", [P, Q], F32) for j in range(2)]
    ps_o = [PSUM(f"e_pso{j}", [P, Q], F32) for j in range(2)]
    jb = kb.jbg
    ps_junk = PSUM("e_psj", [1, 8], F32)

    x2r = x2_ap.rearrange("(ct p) t -> p ct t", p=P)
    dpr = dpart_ap.rearrange("(ct p) t -> p ct t", p=P)
    x4r = x4_ap.rearrange("(ct p) t -> p ct t", p=P)
    wfr = wfc_ap.rearrange("(dt p) f -> p dt f", p=P)
    wpr = wproj_ap.rearrange("(ft p) d -> p ft d", p=P)
    cv, dv = kb.cv, kb.dv

    with nc.Block() as blk:
        @blk.sync
        def _(sync):
            for i in range(NCH):
                if i > 1:
                    sync.wait_ge(e_st, 16 * (i - 1))
                sync.dma_start(out=ysb[i % 2].ap().rearrange("p (ct q) -> p ct q", ct=CT),
                               in_=dpr[:, :, tsl(i, Q)]).then_inc(e_in, 16)
                sync.dma_start(out=xin[i % 2].ap().rearrange("p (ct q) -> p ct q", ct=CT),
                               in_=x2r[:, :, tsl(i, Q)]).then_inc(e_in, 16)
                if i > 0:
                    sync.wait_ge(e_spk, 8 * i)
                    sync.dma_start(out=x4r[:, :, tsl(i - 1, Q)],
                                   in_=xin[(i - 1) % 2].ap().rearrange("p (ct q) -> p ct q", ct=CT)
                                   ).then_inc(e_st, 16)
            sync.wait_ge(e_spk, 8 * NCH)
            sync.dma_start(out=x4r[:, :, tsl(NCH - 1, Q)],
                           in_=xin[(NCH - 1) % 2].ap().rearrange("p (ct q) -> p ct q", ct=CT)
                           ).then_inc(e_st, 16)
            sync.wait_ge(e_st, 16 * NCH)

        @blk.gpsimd
        def _(g):
            g.dma_start(out=wfb.ap().rearrange("p (dt f) -> p dt f", dt=CT),
                        in_=wfr).then_inc(e_w, 16)
            g.dma_start(out=wpb.ap().rearrange("p (ft d) -> p ft d", ft=FT),
                        in_=wpr).then_inc(e_w, 16)
            for i in range(NCH):
                g.wait_ge(e_sd2, i + 1)
                for j, strip in enumerate((mus, rstd)):
                    g.dma_start(out=bcd_ap[2 * (i % 2) + j: 2 * (i % 2) + j + 1, 0:Q],
                                in_=strip[:]).then_inc(e_bc, 16)
                g.wait_ge(e_bc, 64 * i + 32)
                for j, dst in enumerate((mub, rb)):
                    row = bcd_ap[2 * (i % 2) + j: 2 * (i % 2) + j + 1, 0:Q]
                    g.dma_start(out=dst[:], in_=bass.AP(row.tensor, row.offset,
                                                        [[0, P], [1, Q]])).then_inc(e_bc, 16)
                g.wait_ge(e_bc, 64 * (i + 1))
                for ct in range(CT):
                    if 8 * i + ct >= 2:
                        g.wait_ge(e_gb, 8 * i + ct - 1)
                    g.tensor_mul(sq[ct % 2][:], xin[i % 2][:, tsl(ct, Q)], rb[:]
                                 ).then_inc(e_t0, 1)

        @blk.scalar
        def _(s):
            for i in range(NCH):
                for ct in range(CT):
                    s.wait_ge(e_in, 32 * (i + 1))
                    s.wait_ge(e_fin, 8 * i + ct + 1)
                    if i * 8 + ct >= 2:
                        s.wait_ge(e_m2, 8 * i + ct - 1)
                    s.activation(sq[ct % 2][:], xin[i % 2][:, tsl(ct, Q)], AF.Square
                                 ).then_inc(e_sq, 1)
                s.wait_ge(e_sd, i + 1)
                kb.chain("s", s, s.activation(m2s[:], vars_[:], AF.Ln, bias=kb.ep1[:]))
                s.activation(rstd[:], m2s[:], AF.Exp, scale=kb.mh1[:]).then_inc(e_sa, 1)
                for ht in range(FT):
                    s.wait_ge(e_mmh, FT * i + ht + 1)
                    if FT * i + ht >= 2:
                        s.wait_ge(e_q2, FT * i + ht - 1)
                    kb.chain("s", s, s.activation(st_[ht % 2][:], ps_h[ht % 2][:], AF.Relu))
                    s.activation(hbf[:, tsl(ht, Q)], st_[ht % 2][:], AF.Square
                                 ).then_inc(e_q2, 1)
                for dt in range(CT):
                    s.wait_ge(e_mmo, 8 * i + dt + 1)
                    s.activation(ysb[i % 2][:, tsl(dt, Q)], ps_o[dt % 2][:], AF.Copy
                                 ).then_inc(e_ev, 1)

        @blk.tensor
        def _(t):
            for i in range(NCH):
                t.wait_ge(e_sq, 8 * i)
                t.wait_ge(e_fin, 8 * (i + 1) - 0)
                if i >= 1:
                    t.wait_ge(e_sd, i)
                for ct in range(CT):
                    t.matmul(ps_m1[:], kb.c1024[:], xin[i % 2][:, tsl(ct, Q)],
                             start=(ct == 0), stop=(ct == CT - 1))
                for ct in range(CT):
                    t.wait_ge(e_sq, 8 * i + ct + 1)
                    t.matmul(ps_m2[:], kb.c1024[:], sq[ct % 2][:],
                             start=(ct == 0), stop=(ct == CT - 1))
                    t.matmul(ps_junk[:], jb[:, 0:1], jb[:, 0:8],
                             start=True, stop=True).then_inc(e_m2, 1)
                t.wait_ge(e_w, 32)
                t.wait_ge(e_gb, 8 * (i + 1))
                for ht in range(FT):
                    if FT * i + ht >= 2:
                        t.wait_ge(e_q2, FT * i + ht - 1)
                    for ct in range(CT):
                        t.matmul(ps_h[ht % 2][:], wfb[:, ct * F + ht * P: ct * F + (ht + 1) * P],
                                 gbf[:, tsl(ct, Q)], start=(ct == 0), stop=(ct == CT - 1))
                    t.matmul(ps_junk[:], jb[:, 0:1], jb[:, 0:8],
                             start=True, stop=True).then_inc(e_mmh, 1)
                t.wait_ge(e_q2, FT * (i + 1))
                for dt in range(CT):
                    if 8 * i + dt >= 2:
                        t.wait_ge(e_ev, 8 * i + dt - 1)
                    for ht in range(FT):
                        t.matmul(ps_o[dt % 2][:], wpb[:, ht * C + dt * P: ht * C + (dt + 1) * P],
                                 hbf[:, tsl(ht, Q)], start=(ht == 0), stop=(ht == FT - 1))
                    t.matmul(ps_junk[:], jb[:, 0:1], jb[:, 0:8],
                             start=True, stop=True).then_inc(e_mmo, 1)

        @blk.vector
        def _(v):
            for i in range(NCH + 1):
                if i < NCH:
                    v.wait_ge(e_in, 32 * (i + 1))
                    for ct in range(CT):
                        v.tensor_add(xin[i % 2][:, tsl(ct, Q)], xin[i % 2][:, tsl(ct, Q)],
                                     ysb[i % 2][:, tsl(ct, Q)]).then_inc(e_fin, 1)
                    v.wait_ge(e_m2, 8 * (i + 1))
                    CHV = lambda ins: kb.chain("v", v, ins)
                    CHV(v.tensor_copy(mus[:], ps_m1[:]))
                    CHV(v.tensor_copy(m2s[:], ps_m2[:]))
                    CHV(v.tensor_mul(vars_[:], mus[:], mus[:]))
                    v.tensor_sub(vars_[:], m2s[:], vars_[:]).then_inc(e_sd, 1)
                    v.wait_ge(e_sa, i + 1)
                    v.tensor_mul(mus[:], mus[:], rstd[:]).then_inc(e_sd2, 1)
                    if i >= 1:
                        v.wait_ge(e_mmh, FT * i)
                    for ct in range(CT):
                        v.wait_ge(e_t0, 8 * i + ct + 1)
                        v.tensor_sub(gbf[:, tsl(ct, Q)], sq[ct % 2][:], mub[:]
                                     ).then_inc(e_gb, 1)
                if i >= 1:
                    j = i - 1
                    CHV2 = lambda ins: kb.chain("v", v, ins)
                    for dt in range(CT):
                        v.wait_ge(e_ev, 8 * j + dt + 1)
                        y = ysb[j % 2][:, tsl(dt, Q)]
                        a_ = acc[:, tsl(dt, Q)]
                        CHV2(v.tensor_single_scalar(a_, y, 1.0, ALU.is_gt))
                        mprev = y
                        for k in range(4):
                            mb_ = mem[k % 2]
                            CHV2(v.tensor_single_scalar(spk[:], mprev, 1.0, ALU.is_gt))
                            CHV2(v.scalar_tensor_tensor(mb_[:], mprev, 0.9, y, ALU.mult, ALU.add))
                            CHV2(v.tensor_sub(mb_[:], mb_[:], spk[:]))
                            CHV2(v.tensor_single_scalar(spk[:], mb_[:], 1.0, ALU.is_gt))
                            CHV2(v.tensor_add(a_, a_, spk[:]))
                            mprev = mb_[:]
                        v.scalar_tensor_tensor(xin[j % 2][:, tsl(dt, Q)], a_, 0.2,
                                               xin[j % 2][:, tsl(dt, Q)], ALU.mult, ALU.add
                                               ).then_inc(e_spk, 1)


    return es

def emit_phase_B(kb, x_ap, a_ap, x1_ap, wck_ap, wcv_ap, wcr_ap, bcd_ap, NCH=32):
    nc = kb.nc
    Q = 256
    S = kb.S
    b_w, b_in, b_sq, b_m2, b_sd, b_sa, b_sd2, b_bc = (
        S("b_w"), S("b_in"), S("b_sq"), S("b_m2"), S("b_sd"), S("b_sa"),
        S("b_sd2"), S("b_bc"))
    b_xx, b_q2, b_mmk, b_sig, b_mmr, b_mmo, b_g1, b_xs, b_st, b_wu = (
        S("b_xx"), S("b_q2"), S("b_mmk"), S("b_sig"), S("b_mmr"), S("b_mmo"),
        S("b_g1"), S("b_xs"), S("b_st"), S("b_wu"))

    es = contextlib.ExitStack()
    A = lambda name, shape, dt=F32: es.enter_context(nc.sbuf_tensor(name, shape, dt))
    PSUM = lambda name, shape, dt=F32: es.enter_context(nc.psum_tensor(name, shape, dt))
    wckb = A("b_wckb", [P, CT * F], BF16)
    wcvb = A("b_wcvb", [P, FT * C], BF16)
    wcrb = A("b_wcrb", [P, CT * C], BF16)
    ain = A("b_ain", [P, CT * Q], F32)
    xin = A("b_xin", [P, CT * Q], F32)
    xxx2b = A("b_xxx2b", [P, CT * Q], BF16)
    q2 = A("b_q2t", [P, FT * Q], BF16)
    rc = A("b_rc", [P, CT * Q], F32)
    h2t = [A(f"b_h2t{j}", [P, Q + 16], F32) for j in range(2)]
    sq = [A(f"b_sqt{j}", [P, Q], F32) for j in range(2)]
    st_ = [A(f"b_st{j}", [P, Q], F32) for j in range(2)]
    g1t = [A(f"b_g1t{j}", [P, Q], F32) for j in range(2)]
    h2last = A("b_h2last", [P, 8], F32)
    colbuf = A("b_colbuf", [P, 8], F32)
    sqcol = A("b_sqcol", [P, 8], F32)
    mu1b = A("b_mu1b", [P, 1], F32)
    r1b = A("b_r1b", [P, 1], F32)
    w1s = A("b_w1s", [1, 8], F32)
    mus = A("b_mus", [1, Q], F32)
    m2s = A("b_m2s", [1, Q], F32)
    vars_ = A("b_vars", [1, Q], F32)
    rstd = A("b_rstd", [1, Q], F32)
    mub = A("b_mub", [P, Q], F32)
    rb = A("b_rb", [P, Q], F32)
    ps_m1 = PSUM("b_psm1", [1, Q], F32)
    ps_m2 = PSUM("b_psm2", [1, Q], F32)
    ps_k = [PSUM(f"b_psk{j}", [P, Q], F32) for j in range(2)]
    ps_v = [PSUM(f"b_psv{j}", [P, Q], F32) for j in range(2)]
    ps_r = PSUM("b_psr", [P, Q], F32)
    jb = kb.jbg
    ps_junk = PSUM("b_psj", [1, 8], F32)

    xr = x_ap.rearrange("(ct p) t -> p ct t", p=P)
    ar = a_ap.rearrange("(ct p) t -> p ct t", p=P)
    x1r = x1_ap.rearrange("(ct p) t -> p ct t", p=P)
    wckr = wck_ap.rearrange("(ct p) o -> p ct o", p=P)
    wcvr = wcv_ap.rearrange("(ht p) d -> p ht d", p=P)
    wcrr = wcr_ap.rearrange("(ct p) o -> p ct o", p=P)
    cv, dv = kb.cv, kb.dv

    with nc.Block() as blk:
        @blk.sync
        def _(sync):
            for ct in range(CT):
                sync.dma_start(out=colbuf[:, ct:ct + 1], in_=ar[:, ct, T - 1:T],
                               allow_slow_non_contiguous=True).then_inc(b_wu, 16)
            for i in range(NCH):
                if i > 0:
                    sync.wait_ge(b_g1, 8 * i)
                    sync.dma_start(out=x1r[:, :, tsl(i - 1, Q)],
                                   in_=xin.ap().rearrange("p (ct q) -> p ct q", ct=CT)
                                   ).then_inc(b_st, 16)
                if i > 0:
                    sync.wait_ge(b_st, 16 * i)
                sync.dma_start(out=ain.ap().rearrange("p (ct q) -> p ct q", ct=CT),
                               in_=ar[:, :, tsl(i, Q)]).then_inc(b_in, 16)
                sync.dma_start(out=xin.ap().rearrange("p (ct q) -> p ct q", ct=CT),
                               in_=xr[:, :, tsl(i, Q)]).then_inc(b_in, 16)
            sync.wait_ge(b_g1, 8 * NCH)
            sync.dma_start(out=x1r[:, :, tsl(NCH - 1, Q)],
                           in_=xin.ap().rearrange("p (ct q) -> p ct q", ct=CT)
                           ).then_inc(b_st, 16)
            sync.wait_ge(b_st, 16 * NCH)

        @blk.gpsimd
        def _(g):
            g.dma_start(out=wckb.ap().rearrange("p (ct o) -> p ct o", ct=CT),
                        in_=wckr).then_inc(b_w, 16)
            g.dma_start(out=wcvb.ap().rearrange("p (ht d) -> p ht d", ht=FT),
                        in_=wcvr).then_inc(b_w, 16)
            g.dma_start(out=wcrb.ap().rearrange("p (ct o) -> p ct o", ct=CT),
                        in_=wcrr).then_inc(b_w, 16)
            g.wait_ge(b_sd2, 1)
            g.dma_start(out=bcd_ap[0:1, 0:2], in_=w1s[0:1, 0:2]).then_inc(b_bc, 16)
            g.wait_ge(b_bc, 16)
            r0 = bcd_ap[0:1, 0:1]
            g.dma_start(out=mu1b[:], in_=bass.AP(r0.tensor, r0.offset, [[0, P], [1, 1]])
                        ).then_inc(b_bc, 16)
            r1_ = bcd_ap[0:1, 1:2]
            g.dma_start(out=r1b[:], in_=bass.AP(r1_.tensor, r1_.offset, [[0, P], [1, 1]])
                        ).then_inc(b_bc, 16)
            for i in range(NCH):
                g.wait_ge(b_sd2, i + 2)
                for j, strip in enumerate((mus, rstd)):
                    g.dma_start(out=bcd_ap[2 * (i % 2) + 1 + j: 2 * (i % 2) + 2 + j, 0:Q],
                                in_=strip[:]).then_inc(b_bc, 16)
                g.wait_ge(b_bc, 64 * i + 48 + 32)
                for j, dst in enumerate((mub, rb)):
                    row = bcd_ap[2 * (i % 2) + 1 + j: 2 * (i % 2) + 2 + j, 0:Q]
                    g.dma_start(out=dst[:], in_=bass.AP(row.tensor, row.offset,
                                                        [[0, P], [1, Q]])).then_inc(b_bc, 16)
                g.wait_ge(b_bc, 64 * (i + 1) + 48)
                for ct in range(CT):
                    g.wait_ge(b_xx, 8 * i + ct + 1)
                    g.tensor_add(ain[:, tsl(ct, Q)], ain[:, tsl(ct, Q)],
                                 xin[:, tsl(ct, Q)]).then_inc(b_xs, 1)

        @blk.scalar
        def _(s):
            s.wait_ge(b_wu, 16 * CT)
            s.activation(sqcol[:], colbuf[:], AF.Square).then_inc(S("b_wsq"), 1)
            s.wait_ge(b_sd, 1)
            kb.chain("s", s, s.activation(w1s[0:1, 2:3], w1s[0:1, 4:5], AF.Ln, bias=kb.ep1[:]))
            s.activation(w1s[0:1, 1:2], w1s[0:1, 2:3], AF.Exp, scale=kb.mh1[:]
                         ).then_inc(b_sa, 1)
            for i in range(NCH):
                for ct in range(CT):
                    s.wait_ge(b_in, 32 * (i + 1))
                    if i * 8 + ct >= 2:
                        s.wait_ge(b_m2, 8 * i + ct - 1)
                    s.activation(sq[ct % 2][:], ain[:, tsl(ct, Q)], AF.Square
                                 ).then_inc(b_sq, 1)
                s.wait_ge(b_sd, i + 2)
                kb.chain("s", s, s.activation(m2s[:], vars_[:], AF.Ln, bias=kb.ep1[:]))
                s.activation(rstd[:], m2s[:], AF.Exp, scale=kb.mh1[:]).then_inc(b_sa, 1)
                for ot in range(FT):
                    s.wait_ge(b_mmk, FT * i + ot + 1)
                    if FT * i + ot >= 2:
                        s.wait_ge(b_q2, FT * i + ot - 1)
                    kb.chain("s", s, s.activation(st_[ot % 2][:], ps_k[ot % 2][:], AF.Relu))
                    s.activation(q2[:, tsl(ot, Q)], st_[ot % 2][:], AF.Square
                                 ).then_inc(b_q2, 1)
                for ot in range(CT):
                    s.wait_ge(b_mmr, 8 * i + ot + 1)
                    s.activation(rc[:, tsl(ot, Q)], ps_r[:], AF.Sigmoid).then_inc(b_sig, 1)

        @blk.tensor
        def _(t):
            t.wait_ge(b_wu, 16 * CT)
            t.matmul(ps_m1[0:1, 0:8], kb.c1024[:], colbuf[:], start=True, stop=True)
            t.wait_ge(S("b_wsq"), 1)
            t.matmul(ps_m1[0:1, 8:16], kb.c1024[:], sqcol[:], start=True, stop=True)
            t.matmul(ps_junk[:], jb[:, 0:1], jb[:, 0:8], start=True, stop=True
                     ).then_inc(b_m2, 1)
            for i in range(NCH):
                t.wait_ge(b_in, 32 * (i + 1))
                t.wait_ge(b_sd, i + 1)
                for ct in range(CT):
                    t.matmul(ps_m1[:], kb.c1024[:], ain[:, tsl(ct, Q)],
                             start=(ct == 0), stop=(ct == CT - 1))
                for ct in range(CT):
                    t.wait_ge(b_sq, 8 * i + ct + 1)
                    t.matmul(ps_m2[:], kb.c1024[:], sq[ct % 2][:],
                             start=(ct == 0), stop=(ct == CT - 1))
                    t.matmul(ps_junk[:], jb[:, 0:1], jb[:, 0:8],
                             start=True, stop=True).then_inc(b_m2, 1)
                t.wait_ge(b_w, 48)
                t.wait_ge(b_xx, 8 * (i + 1))
                for ot in range(FT):
                    if FT * i + ot >= 2:
                        t.wait_ge(b_q2, FT * i + ot - 1)
                    for ct in range(CT):
                        t.matmul(ps_k[ot % 2][:], wckb[:, ct * F + ot * P: ct * F + (ot + 1) * P],
                                 xxx2b[:, tsl(ct, Q)], start=(ct == 0), stop=(ct == CT - 1))
                    t.matmul(ps_junk[:], jb[:, 0:1], jb[:, 0:8],
                             start=True, stop=True).then_inc(b_mmk, 1)
                for ot in range(CT):
                    if 8 * i + ot >= 1:
                        t.wait_ge(b_sig, 8 * i + ot)
                    for ct in range(CT):
                        t.matmul(ps_r[:], wcrb[:, ct * C + ot * P: ct * C + (ot + 1) * P],
                                 xxx2b[:, tsl(ct, Q)], start=(ct == 0), stop=(ct == CT - 1))
                    t.matmul(ps_junk[:], jb[:, 0:1], jb[:, 0:8],
                             start=True, stop=True).then_inc(b_mmr, 1)
                t.wait_ge(b_q2, FT * (i + 1))
                for dt in range(CT):
                    if 8 * i + dt >= 2:
                        t.wait_ge(b_g1, 8 * i + dt - 1)
                    for ht in range(FT):
                        t.matmul(ps_v[dt % 2][:], wcvb[:, ht * C + dt * P: ht * C + (dt + 1) * P],
                                 q2[:, tsl(ht, Q)], start=(ht == 0), stop=(ht == FT - 1))
                    t.matmul(ps_junk[:], jb[:, 0:1], jb[:, 0:8],
                             start=True, stop=True).then_inc(b_mmo, 1)

        @blk.vector
        def _(v):
            CHV = lambda ins: kb.chain("v", v, ins)
            v.wait_ge(b_m2, 1)
            CHV(v.reduce_sum(w1s[0:1, 3:4], ps_m1[0:1, 0:8], axis=mybir.AxisListType.X))
            CHV(v.reduce_sum(w1s[0:1, 5:6], ps_m1[0:1, 8:16], axis=mybir.AxisListType.X))
            CHV(v.tensor_mul(w1s[0:1, 6:7], w1s[0:1, 3:4], w1s[0:1, 3:4]))
            v.tensor_sub(w1s[0:1, 4:5], w1s[0:1, 5:6], w1s[0:1, 6:7]).then_inc(b_sd, 1)
            v.wait_ge(b_sa, 1)
            CHV(v.tensor_mul(w1s[0:1, 0:1], w1s[0:1, 3:4], w1s[0:1, 1:2]))
            v.tensor_copy(w1s[0:1, 6:7], w1s[0:1, 0:1]).then_inc(b_sd2, 1)
            v.wait_ge(b_bc, 48)
            for ct in range(CT):
                CHV(v.tensor_sub(h2t[0][:, 0:1], colbuf[:, ct:ct + 1], mu1b[:]))
                CHV(v.tensor_mul(h2t[0][:, 1:2], h2t[0][:, 0:1], r1b[:]))
                CHV(v.tensor_scalar(h2last[:, ct:ct + 1], h2t[0][:, 1:2],
                                    cv[:, CV_LN2W + ct:CV_LN2W + ct + 1],
                                    cv[:, CV_LN2B + ct:CV_LN2B + ct + 1],
                                    ALU.mult, ALU.add))
            for i in range(NCH):
                v.wait_ge(b_m2, 8 * (i + 1) + 1)
                CHV(v.tensor_copy(mus[:], ps_m1[:]))
                CHV(v.tensor_copy(m2s[:], ps_m2[:]))
                CHV(v.tensor_mul(vars_[:], mus[:], mus[:]))
                v.tensor_sub(vars_[:], m2s[:], vars_[:]).then_inc(b_sd, 1)
                v.wait_ge(b_sa, i + 2)
                v.tensor_mul(mus[:], mus[:], rstd[:]).then_inc(b_sd2, 1)
                v.wait_ge(b_bc, 64 * (i + 1) + 48)
                for ct in range(CT):
                    hb = h2t[ct % 2]
                    CHV(v.tensor_tensor(hb[:, 15:16], h2last[:, ct:ct + 1],
                                        h2last[:, ct:ct + 1], ALU.max))
                    CHV(v.tensor_mul(sq[ct % 2][:], ain[:, tsl(ct, Q)], rb[:]))
                    CHV(v.tensor_sub(st_[ct % 2][:], sq[ct % 2][:], mub[:]))
                    CHV(v.tensor_scalar(hb[:, 16:Q + 16], st_[ct % 2][:],
                                        cv[:, CV_LN2W + ct:CV_LN2W + ct + 1],
                                        cv[:, CV_LN2B + ct:CV_LN2B + ct + 1],
                                        ALU.mult, ALU.add), )
                    CHV(v.tensor_tensor(h2last[:, ct:ct + 1], hb[:, Q + 15:Q + 16],
                                        hb[:, Q + 15:Q + 16], ALU.max))
                    CHV(v.tensor_scalar_mul(g1t[ct % 2][:], hb[:, 16:Q + 16],
                                            dv[:, DV_ONEMTS + ct:DV_ONEMTS + ct + 1]))
                    v.scalar_tensor_tensor(xxx2b[:, tsl(ct, Q)], hb[:, 15:Q + 15],
                                           cv[:, CV_TS + ct:CV_TS + ct + 1],
                                           g1t[ct % 2][:], ALU.mult, ALU.add
                                           ).then_inc(b_xx, 1)
                for dt in range(CT):
                    v.wait_ge(b_mmo, 8 * i + dt + 1)
                    v.wait_ge(b_sig, 8 * (i + 1))
                    CHV(v.tensor_mul(g1t[dt % 2][:], rc[:, tsl(dt, Q)], ps_v[dt % 2][:]))
                    v.wait_ge(b_xs, 8 * i + dt + 1)
                    v.tensor_add(xin[:, tsl(dt, Q)], ain[:, tsl(dt, Q)],
                                 g1t[dt % 2][:]).then_inc(b_g1, 1)



    return es

def emit_phase_A(kb, x_ap, a_out_ap, wk_ap, wv_ap, wr_ap, bcd_ap, NCH=16):
    nc = kb.nc
    Q = 512
    S = kb.S
    a_w, a_in, a_sq, a_m2, a_sd, a_sa, a_sd2, a_sa2, a_sd3, a_bc = (
        S("a_w"), S("a_in"), S("a_sq"), S("a_m2"), S("a_sd"), S("a_sa"),
        S("a_sd2"), S("a_sa2"), S("a_sd3"), S("a_bc"))
    a_xx, a_mmk, a_mmv, a_mmr, a_evE, a_evr, a_evv, a_fin, a_st, _aju = (
        S("a_xx"), S("a_mmk"), S("a_mmv"), S("a_mmr"), S("a_evE"), S("a_evr"),
        S("a_evv"), S("a_fin"), S("a_st"), S("a_st"))

    es = contextlib.ExitStack()
    A = lambda name, shape, dt=F32: es.enter_context(nc.sbuf_tensor(name, shape, dt))
    PSUM = lambda name, shape, dt=F32: es.enter_context(nc.psum_tensor(name, shape, dt))
    wkb = A("a_wkb", [P, CT * C], BF16)
    wvb = A("a_wvb", [P, CT * C], BF16)
    wrb = A("a_wrb", [P, CT * C], BF16)
    xin = A("a_xin", [P, CT * Q], F32)
    xn = A("a_xn", [P, CT * Q], F32)
    hbuf = A("a_hbuf", [P, CT * (Q + 16)], F32)
    xxxb = A("a_xxxb", [P, CT * Q], BF16)
    sq = [A(f"a_sq{j}", [P, Q], F32) for j in range(2)]
    sc2 = [A(f"a_sc2{j}", [P, Q], F32) for j in range(2)]
    Ebuf = [A(f"a_E{j}", [P, Q], F32) for j in range(2)]
    rbuf = [A(f"a_r{j}", [P, Q], F32) for j in range(2)]
    Ev = A("a_Ev", [P, Q], F32)
    Ibuf = A("a_I", [P, Q], F32)
    Dprev = A("a_Dp", [P, Q], F32)
    Rm = A("a_Rm", [P, Q], F32)
    alp = A("a_alp", [P, Q], F32)
    bet = A("a_bet", [P, Q], F32)
    aab = A("a_aab", [P, Q + 16], F32)
    num = A("a_num", [P, Q], F32)
    den = A("a_den", [P, Q], F32)
    ones512 = A("a_ones", [P, Q], F32)
    carryI = A("a_cI", [P, 8], F32)
    carryA = A("a_cA", [P, 8], F32)
    mus = A("a_mus", [1, Q], F32)
    m2s = A("a_m2s", [1, Q], F32)
    vars_ = A("a_vars", [1, Q], F32)
    rstd = A("a_rstd", [1, Q], F32)
    r1s = A("a_r1s", [1, Q], F32)
    scr = A("a_scr", [1, Q], F32)
    mrsb = A("a_mrsb", [P, Q], F32)
    r0b = A("a_r0b", [P, Q], F32)
    r1b = A("a_r1b", [P, Q], F32)
    ps_m1 = PSUM("a_psm1", [1, Q], F32)
    ps_m2 = PSUM("a_psm2", [1, Q], F32)
    ps_k = PSUM("a_psk", [P, Q], F32)
    ps_v = PSUM("a_psv", [P, Q], F32)
    ps_r = PSUM("a_psr", [P, Q], F32)
    jb = kb.jbg
    ps_junk = PSUM("a_psj", [1, 8], F32)

    xr = x_ap.rearrange("(ct p) t -> p ct t", p=P)
    aor = a_out_ap.rearrange("(ct p) t -> p ct t", p=P)
    wkr = wk_ap.rearrange("(ct p) o -> p ct o", p=P)
    wvr = wv_ap.rearrange("(ct p) o -> p ct o", p=P)
    wrr = wr_ap.rearrange("(ct p) o -> p ct o", p=P)
    cv, dv = kb.cv, kb.dv

    with nc.Block() as blk:
        @blk.sync
        def _(sync):
            for i in range(NCH):
                if i > 0:
                    sync.wait_ge(a_fin, 8 * i)
                    sync.dma_start(out=aor[:, :, tsl(i - 1, Q)],
                                   in_=xin.ap().rearrange("p (ct q) -> p ct q", ct=CT)
                                   ).then_inc(a_st, 16)
                if i > 0:
                    sync.wait_ge(a_st, 16 * i)
                sync.dma_start(out=xin.ap().rearrange("p (ct q) -> p ct q", ct=CT),
                               in_=xr[:, :, tsl(i, Q)]).then_inc(a_in, 16)
            sync.wait_ge(a_fin, 8 * NCH)
            sync.dma_start(out=aor[:, :, tsl(NCH - 1, Q)],
                           in_=xin.ap().rearrange("p (ct q) -> p ct q", ct=CT)
                           ).then_inc(a_st, 16)
            sync.wait_ge(a_st, 16 * NCH)

        @blk.gpsimd
        def _(g):
            g.dma_start(out=wkb.ap().rearrange("p (ct o) -> p ct o", ct=CT),
                        in_=wkr).then_inc(a_w, 16)
            g.dma_start(out=wvb.ap().rearrange("p (ct o) -> p ct o", ct=CT),
                        in_=wvr).then_inc(a_w, 16)
            g.dma_start(out=wrb.ap().rearrange("p (ct o) -> p ct o", ct=CT),
                        in_=wrr).then_inc(a_w, 16)
            for i in range(NCH):
                g.wait_ge(a_sd3, i + 1)
                for j, strip in enumerate((mus, rstd, r1s)):
                    g.dma_start(out=bcd_ap[3 * (i % 2) + j: 3 * (i % 2) + j + 1, 0:Q],
                                in_=strip[:]).then_inc(a_bc, 16)
                g.wait_ge(a_bc, 96 * i + 48)
                for j, dst in enumerate((mrsb, r0b, r1b)):
                    row = bcd_ap[3 * (i % 2) + j: 3 * (i % 2) + j + 1, 0:Q]
                    g.dma_start(out=dst[:], in_=bass.AP(row.tensor, row.offset,
                                                        [[0, P], [1, Q]])).then_inc(a_bc, 16)

        @blk.scalar
        def _(s):
            for i in range(NCH):
                for ct in range(CT):
                    s.wait_ge(a_in, 16 * (i + 1))
                    if i * 8 + ct >= 2:
                        s.wait_ge(a_m2, 8 * i + ct - 1)
                    s.activation(sq[ct % 2][:], xin[:, tsl(ct, Q)], AF.Square
                                 ).then_inc(a_sq, 1)
                s.wait_ge(a_sd, i + 1)
                kb.chain("s", s, s.activation(scr[:], vars_[:], AF.Ln, bias=kb.ep1[:]))
                s.activation(rstd[:], scr[:], AF.Exp, scale=kb.mh1[:]).then_inc(a_sa, 1)
                s.wait_ge(a_sd2, i + 1)
                kb.chain("s", s, s.activation(scr[:], vars_[:], AF.Ln, bias=kb.ep1[:]))
                s.activation(r1s[:], scr[:], AF.Exp, scale=kb.mh1[:]).then_inc(a_sa2, 1)
                for ot in range(CT):
                    idx = 8 * i + ot
                    s.wait_ge(a_mmk, idx + 1)
                    if idx >= 2:
                        s.wait_ge(a_fin, idx - 1)
                    s.activation(Ebuf[ot % 2][:], ps_k[:], AF.Exp).then_inc(a_evE, 1)
                    s.wait_ge(a_mmr, idx + 1)
                    s.activation(rbuf[ot % 2][:], ps_r[:], AF.Sigmoid).then_inc(a_evr, 1)

        @blk.tensor
        def _(t):
            for i in range(NCH):
                t.wait_ge(a_in, 16 * (i + 1))
                if i >= 1:
                    t.wait_ge(a_sd, i)
                for ct in range(CT):
                    t.matmul(ps_m1[:], kb.c1024[:], xin[:, tsl(ct, Q)],
                             start=(ct == 0), stop=(ct == CT - 1))
                for ct in range(CT):
                    t.wait_ge(a_sq, 8 * i + ct + 1)
                    t.matmul(ps_m2[:], kb.c1024[:], sq[ct % 2][:],
                             start=(ct == 0), stop=(ct == CT - 1))
                    t.matmul(ps_junk[:], jb[:, 0:1], jb[:, 0:8],
                             start=True, stop=True).then_inc(a_m2, 1)
                t.wait_ge(a_w, 48)
                t.wait_ge(a_xx, 8 * (i + 1))
                for ot in range(CT):
                    if 8 * i + ot >= 1:
                        t.wait_ge(a_evE, 8 * i + ot)
                    for ct in range(CT):
                        t.matmul(ps_k[:], wkb[:, ct * C + ot * P: ct * C + (ot + 1) * P],
                                 xxxb[:, tsl(ct, Q)], start=(ct == 0), stop=(ct == CT - 1))
                    t.matmul(ps_junk[:], jb[:, 0:1], jb[:, 0:8],
                             start=True, stop=True).then_inc(a_mmk, 1)
                    if 8 * i + ot >= 1:
                        t.wait_ge(a_evv, 8 * i + ot)
                    for ct in range(CT):
                        t.matmul(ps_v[:], wvb[:, ct * C + ot * P: ct * C + (ot + 1) * P],
                                 xxxb[:, tsl(ct, Q)], start=(ct == 0), stop=(ct == CT - 1))
                    t.matmul(ps_junk[:], jb[:, 0:1], jb[:, 0:8],
                             start=True, stop=True).then_inc(a_mmv, 1)
                    if 8 * i + ot >= 1:
                        t.wait_ge(a_evr, 8 * i + ot)
                    for ct in range(CT):
                        t.matmul(ps_r[:], wrb[:, ct * C + ot * P: ct * C + (ot + 1) * P],
                                 xxxb[:, tsl(ct, Q)], start=(ct == 0), stop=(ct == CT - 1))
                    t.matmul(ps_junk[:], jb[:, 0:1], jb[:, 0:8],
                             start=True, stop=True).then_inc(a_mmr, 1)

        @blk.vector
        def _(v):
            CHV = lambda ins: kb.chain("v", v, ins)
            v.memset(ones512[:], 1.0)
            v.memset(carryI[:], 0.0)
            v.memset(carryA[:], 0.0)
            v.memset(aab[:, 15:16], 0.0)
            for ct in range(CT):
                CHV(v.tensor_tensor(hbuf[:, ct * (Q + 16) + 15: ct * (Q + 16) + 16],
                                    cv[:, CV_TS + ct:CV_TS + ct + 1],
                                    cv[:, CV_TS + ct:CV_TS + ct + 1], ALU.max))
            for i in range(NCH):
                v.wait_ge(a_m2, 8 * (i + 1))
                CHV(v.tensor_copy(mus[:], ps_m1[:]))
                CHV(v.tensor_copy(m2s[:], ps_m2[:]))
                CHV(v.tensor_mul(vars_[:], mus[:], mus[:]))
                v.tensor_sub(vars_[:], m2s[:], vars_[:]).then_inc(a_sd, 1)
                v.wait_ge(a_sa, i + 1)
                CHV(v.tensor_mul(m2s[:], vars_[:], rstd[:]))
                v.tensor_mul(vars_[:], m2s[:], rstd[:]).then_inc(a_sd2, 1)
                v.wait_ge(a_sa2, i + 1)
                v.tensor_mul(mus[:], mus[:], rstd[:]).then_inc(a_sd3, 1)
                v.wait_ge(a_bc, 96 * (i + 1))
                for ct in range(CT):
                    hb = hbuf[:, ct * (Q + 16):(ct + 1) * (Q + 16)]
                    if i > 0:
                        CHV(v.tensor_tensor(hb[:, 15:16], hb[:, Q + 15:Q + 16],
                                            hb[:, Q + 15:Q + 16], ALU.max))
                    CHV(v.tensor_mul(sq[ct % 2][:], xin[:, tsl(ct, Q)], r0b[:]))
                    CHV(v.tensor_sub(xn[:, tsl(ct, Q)], sq[ct % 2][:], mrsb[:]))
                    CHV(v.tensor_mul(sc2[ct % 2][:], xn[:, tsl(ct, Q)], r1b[:]))
                    CHV(v.tensor_scalar(hb[:, 16:Q + 16], sc2[ct % 2][:],
                                        cv[:, CV_LN1W + ct:CV_LN1W + ct + 1],
                                        cv[:, CV_LN1B + ct:CV_LN1B + ct + 1],
                                        ALU.mult, ALU.add))
                    CHV(v.tensor_scalar_mul(sc2[ct % 2][:], hb[:, 16:Q + 16],
                                            dv[:, DV_TS1 + ct:DV_TS1 + ct + 1]))
                    v.scalar_tensor_tensor(xxxb[:, tsl(ct, Q)], hb[:, 15:Q + 15],
                                           dv[:, DV_NEGTS + ct:DV_NEGTS + ct + 1],
                                           sc2[ct % 2][:], ALU.mult, ALU.add
                                           ).then_inc(a_xx, 1)
                for ot in range(CT):
                    idx = 8 * i + ot
                    v.wait_ge(a_evE, idx + 1)
                    v.wait_ge(a_mmv, idx + 1)
                    E = Ebuf[ot % 2]
                    v.tensor_mul(Ev[:], E[:], ps_v[:]).then_inc(a_evv, 1)
                    CHV(v.tensor_tensor_scan(Ibuf[:], ones512[:], E[:],
                                             carryI[:, ot:ot + 1], ALU.mult, ALU.add))
                    CHV(v.tensor_tensor(carryI[:, ot:ot + 1], Ibuf[:, Q - 1:Q],
                                        Ibuf[:, Q - 1:Q], ALU.max))
                    v.scalar_tensor_tensor(Dprev[:], Ibuf[:], 1.0, E[:],
                                           ALU.add, ALU.subtract)
                    v.tensor_max(num[:], Dprev[:], E[:])
                    v.reciprocal(Rm[:], num[:])
                    v.tensor_mul(alp[:], Dprev[:], Rm[:])
                    v.tensor_mul(bet[:], Ev[:], Rm[:])
                    CHV(v.tensor_tensor(aab[:, 15:16], carryA[:, ot:ot + 1],
                                        carryA[:, ot:ot + 1], ALU.max))
                    CHV(v.tensor_tensor_scan(aab[:, 16:Q + 16], alp[:], bet[:],
                                             carryA[:, ot:ot + 1], ALU.mult, ALU.add))
                    CHV(v.tensor_tensor(carryA[:, ot:ot + 1], aab[:, Q + 15:Q + 16],
                                        aab[:, Q + 15:Q + 16], ALU.max))
                    v.tensor_mul(num[:], Dprev[:], aab[:, 15:Q + 15])
                    v.scalar_tensor_tensor(num[:], Ev[:], dv[:, DV_U + ot:DV_U + ot + 1],
                                           num[:], ALU.mult, ALU.add)
                    v.scalar_tensor_tensor(den[:], E[:], dv[:, DV_U + ot:DV_U + ot + 1],
                                           Dprev[:], ALU.mult, ALU.add)
                    v.reciprocal(bet[:], den[:])
                    v.tensor_mul(num[:], num[:], bet[:])
                    v.wait_ge(a_evr, idx + 1)
                    v.tensor_mul(num[:], rbuf[ot % 2][:], num[:])
                    v.tensor_add(xin[:, tsl(ot, Q)], xn[:, tsl(ot, Q)], num[:]
                                 ).then_inc(a_fin, 1)


    return es

def emit_phase_D(kb, x2_ap, dpart_ap, wxp_ap, wdt_ap, bcdD_ap, NCH=8, dbg=None):
    nc = kb.nc
    Q = 1024
    H = 512
    S = kb.S
    d_w, d_in, d_sq, d_m2, d_sev, d_sd, d_sa, d_sd2, d_bc = (
        S("d_w"), S("d_in"), S("d_sq"), S("d_m2"), S("d_sev"), S("d_sd"),
        S("d_sa"), S("d_sd2"), S("d_bc"))
    d_ap, d_gb, d_mmx, d_evx, d_mmd, d_evd, d_dg, d_bcn, d_aex, d_scn, d_fin, d_st, d_mmg = (
        S("d_ap"), S("d_gb"), S("d_mmx"), S("d_evx"), S("d_mmd"), S("d_evd"),
        S("d_dg"), S("d_bcn"), S("d_aex"), S("d_scn"), S("d_fin"), S("d_st"), S("d_mmg"))

    es = contextlib.ExitStack()
    A = lambda name, shape, dt=F32: es.enter_context(nc.sbuf_tensor(name, shape, dt))
    PSUM = lambda name, shape, dt=F32: es.enter_context(nc.psum_tensor(name, shape, dt))
    wxpb = A("d_wxpb", [P, CT * 96], BF16)
    wdtb = A("d_wdtb", [64, CT * P], BF16)
    xin = A("d_xin", [P, CT * Q], F32)
    g = A("d_g", [P, CT * Q], F32)
    delta = A("d_delta", [P, CT * Q], F32)
    gbt = [A(f"d_gbt{j}", [P, H], BF16) for j in range(2)]
    at = [A(f"d_at{j}", [P, Q], F32) for j in range(2)]
    ut = [A(f"d_ut{j}", [P, Q], F32) for j in range(2)]
    yt = A("d_yt", [P, Q], F32)
    ht = [A("d_ht0", [P, Q], F32)] * 2
    Brep = [A(f"d_Brep{j}", [P, Q], F32) for j in range(2)]
    Crep = [A(f"d_Crep{j}", [P, Q], F32) for j in range(2)]
    Bsb = A("d_Bsb", [16, Q], F32)
    Csb = A("d_Csb", [16, Q], F32)
    dpre_bf = A("d_dpre", [64, Q], BF16)
    Ed = [A("d_Ed0", [P, H], F32)] * 2
    sq = [A(f"d_sqt{j}", [P, H], F32) for j in range(2)]
    hstate = A("d_hstate", [P, 128], F32)
    mus = A("d_mus", [1, H], F32)
    m2s = A("d_m2s", [1, H], F32)
    vars_ = A("d_vars", [1, H], F32)
    rstd = A("d_rstd", [1, H], F32)
    mub = A("d_mub", [P, Q], F32)
    rb = A("d_rb", [P, Q], F32)
    ps_m1 = PSUM("d_psm1", [1, H], F32)
    ps_m2 = PSUM("d_psm2", [1, H], F32)
    ps_dp = PSUM("d_psdp", [64, H], F32)
    ps_B = PSUM("d_psB", [16, H], F32)
    ps_C = PSUM("d_psC", [16, H], F32)
    ps_dr = [PSUM(f"d_psdr{j}", [P, H], F32) for j in range(2)]
    jb = kb.jbg
    ps_junk = PSUM("d_psj", [1, 8], F32)

    x2r = x2_ap.rearrange("(ct p) t -> p ct t", p=P)
    dpr = dpart_ap.rearrange("(ct p) t -> p ct t", p=P)
    wxr = wxp_ap.rearrange("(ct p) e -> p ct e", p=P)
    wdr = wdt_ap.rearrange("r (dt p) -> r dt p", p=P)
    cv, dv = kb.cv, kb.dv

    with nc.Block() as blk:
        @blk.sync
        def _(sync):
            for i in range(NCH):
                if i > 0:
                    sync.wait_ge(d_fin, 8 * i)
                    sync.dma_start(out=dpr[:, :, tsl(i - 1, Q)],
                                   in_=xin.ap().rearrange("p (ct q) -> p ct q", ct=CT)
                                   ).then_inc(d_st, 16)
                if i > 0:
                    sync.wait_ge(d_st, 16 * i)
                sync.dma_start(out=xin.ap().rearrange("p (ct q) -> p ct q", ct=CT),
                               in_=x2r[:, :, tsl(i, Q)]).then_inc(d_in, 16)
            sync.wait_ge(d_fin, 8 * NCH)
            sync.dma_start(out=dpr[:, :, tsl(NCH - 1, Q)],
                           in_=xin.ap().rearrange("p (ct q) -> p ct q", ct=CT)
                           ).then_inc(d_st, 16)
            sync.wait_ge(d_st, 16 * NCH)
            if dbg is not None:
                nd = 0
                for name, src in (("delta", delta), ("g", g), ("dg", dg),
                                  ("ht", ht[0]), ("at", at[1]), ("Bsb", Bsb), ("Csb", Csb), ("mub", mub), ("rb", rb)):
                    if name in dbg:
                        sync.dma_start(out=dbg[name], in_=src[:]).then_inc(d_st, 16)
                        nd += 1
                sync.wait_ge(d_st, 16 * (NCH + nd))

        @blk.gpsimd
        def _(g_):
            g_.dma_start(out=wxpb.ap().rearrange("p (ct e) -> p ct e", ct=CT),
                         in_=wxr).then_inc(d_w, 16)
            g_.dma_start(out=wdtb.ap().rearrange("r (dt p) -> r dt p", dt=CT),
                         in_=wdr).then_inc(d_w, 16)
            for i in range(NCH):
                for hf in range(2):
                    g_.wait_ge(d_sd2, 2 * i + hf + 1)
                    r0 = 36 + 4 * (i % 2) + 2 * hf
                    for j, strip in enumerate((mus, rstd)):
                        g_.dma_start(out=bcdD_ap[r0 + j: r0 + j + 1, 0:H],
                                     in_=strip[:]).then_inc(d_bc, 16)
                    g_.wait_ge(d_bc, 160 * i + 64 * hf + 32)
                    for j, dst in enumerate((mub, rb)):
                        row = bcdD_ap[r0 + j: r0 + j + 1, 0:H]
                        g_.dma_start(out=dst[:, tsl(hf, H)],
                                     in_=bass.AP(row.tensor, row.offset, [[0, P], [1, H]])
                                     ).then_inc(d_bc, 16)
                g_.wait_ge(d_evx, 4 * (i + 1))
                base = 18 * (i % 2)
                g_.dma_start(out=bcdD_ap[base:base + 16, 0:Q], in_=Bsb[:]).then_inc(d_bc, 16)
                g_.dma_start(out=bcdD_ap[72 + base:72 + base + 16, 0:Q],
                             in_=Csb[:]).then_inc(d_bc, 16)
                g_.wait_ge(d_bc, 160 * (i + 1))
                for n in range(16):
                    w = 128 * i + 8 * (n - 1)
                    if n < 2:
                        w = 128 * i
                    if w > 0:
                        g_.wait_ge(d_scn, w)
                    rowB = bcdD_ap[base + n:base + n + 1, 0:Q]
                    rowC = bcdD_ap[72 + base + n:72 + base + n + 1, 0:Q]
                    g_.dma_start(out=Brep[n % 2][:],
                                 in_=bass.AP(rowB.tensor, rowB.offset, [[0, P], [1, Q]])
                                 ).then_inc(d_bcn, 16)
                    g_.dma_start(out=Crep[n % 2][:],
                                 in_=bass.AP(rowC.tensor, rowC.offset, [[0, P], [1, Q]])
                                 ).then_inc(d_bcn, 16)

        @blk.scalar
        def _(s):
            for i in range(NCH):
                s.wait_ge(d_in, 16 * (i + 1))
                for hf in range(2):
                    for ct in range(CT):
                        idx = 16 * i + 8 * hf + ct
                        if idx >= 2:
                            s.wait_ge(d_m2, idx - 1)
                        s.activation(sq[ct % 2][:], xin[:, ct * Q + hf * H: ct * Q + (hf + 1) * H],
                                     AF.Square).then_inc(d_sq, 1)
                for hf in range(2):
                    s.wait_ge(d_sd, 2 * i + hf + 1)
                    kb.chain("s", s, s.activation(m2s[:], vars_[:], AF.Ln, bias=kb.ep1[:]))
                    s.activation(rstd[:], m2s[:], AF.Exp, scale=kb.mh1[:]).then_inc(d_sa, 1)
                for hf in range(2):
                    s.wait_ge(d_mmx, 2 * i + hf + 1)
                    s.activation(dpre_bf[:, tsl(hf, H)], ps_dp[:], AF.Copy
                                 ).then_inc(d_evx, 1)
                    for dt in range(CT):
                        idx = 16 * i + 8 * hf + dt
                        s.wait_ge(d_mmd, idx + 1)
                        kb.chain("s", s, s.activation(Ed[dt % 2][:], ps_dr[dt % 2][:], AF.Exp,
                                                      bias=cv[:, CV_BDT + dt:CV_BDT + dt + 1]))
                        s.activation(delta[:, dt * Q + hf * H: dt * Q + (hf + 1) * H],
                                     Ed[dt % 2][:], AF.Ln, bias=kb.one_pp[:]).then_inc(d_evd, 1)
                for n in range(16):
                    for dt in range(CT):
                        idx = 128 * i + 8 * n + dt
                        if idx >= 2:
                            s.wait_ge(d_scn, idx - 1)
                        s.activation(at[idx % 2][:], delta[:, tsl(dt, Q)], AF.Exp,
                                     scale=dv[:, DV_ANEG + dt * 16 + n:DV_ANEG + dt * 16 + n + 1]
                                     ).then_inc(d_aex, 1)

        @blk.tensor
        def _(t):
            for i in range(NCH):
                t.wait_ge(d_in, 16 * (i + 1))
                if i >= 1:
                    t.wait_ge(d_sev, 2 * i)
                for hf in range(2):
                    if hf == 1:
                        t.wait_ge(d_sev, 2 * i + 1)
                    for ct in range(CT):
                        t.matmul(ps_m1[:], kb.c1024[:], xin[:, ct * Q + hf * H: ct * Q + (hf + 1) * H],
                                 start=(ct == 0), stop=(ct == CT - 1))
                    for ct in range(CT):
                        t.wait_ge(d_sq, 16 * i + 8 * hf + ct + 1)
                        t.matmul(ps_m2[:], kb.c1024[:], sq[ct % 2][:],
                                 start=(ct == 0), stop=(ct == CT - 1))
                        t.matmul(ps_junk[:], jb[:, 0:1], jb[:, 0:8],
                                 start=True, stop=True).then_inc(d_m2, 1)
                t.wait_ge(d_w, 32)
                t.wait_ge(d_ap, 8 * (i + 1))
                for hf in range(2):
                    if 2 * i + hf >= 1:
                        t.wait_ge(d_evx, 2 * (2 * i + hf))
                    for ct in range(CT):
                        t.wait_ge(d_gb, 16 * i + 8 * hf + ct + 1)
                        gb_ = gbt[ct % 2]
                        t.matmul(ps_dp[:], wxpb[:, ct * 96: ct * 96 + 64], gb_[:],
                                 start=(ct == 0), stop=(ct == CT - 1))
                        t.matmul(ps_B[:], wxpb[:, ct * 96 + 64: ct * 96 + 80], gb_[:],
                                 start=(ct == 0), stop=(ct == CT - 1))
                        t.matmul(ps_C[:], wxpb[:, ct * 96 + 80: ct * 96 + 96], gb_[:],
                                 start=(ct == 0), stop=(ct == CT - 1))
                        t.matmul(ps_junk[:], jb[:, 0:1], jb[:, 0:8],
                                 start=True, stop=True).then_inc(d_mmg, 1)
                    t.matmul(ps_junk[:], jb[:, 0:1], jb[:, 0:8],
                             start=True, stop=True).then_inc(d_mmx, 1)
                    t.wait_ge(d_evx, 2 * (2 * i + hf) + 1)
                    for dt in range(CT):
                        idx = 16 * i + 8 * hf + dt
                        if idx >= 2:
                            t.wait_ge(d_evd, idx - 1)
                        t.matmul(ps_dr[dt % 2][:], wdtb[:, tsl(dt, P)], dpre_bf[:, tsl(hf, H)],
                                 start=True, stop=True)
                        t.matmul(ps_junk[:], jb[:, 0:1], jb[:, 0:8],
                                 start=True, stop=True).then_inc(d_mmd, 1)

        @blk.vector
        def _(v):
            CHV = lambda ins: kb.chain("v", v, ins)
            v.memset(hstate[:], 0.0)
            for i in range(NCH):
                for hf in range(2):
                    v.wait_ge(d_m2, 16 * i + 8 * hf + 8)
                    if hf == 1:
                        v.wait_ge(d_bc, 160 * i + 32)
                    CHV(v.tensor_copy(mus[:], ps_m1[:]))
                    v.tensor_copy(m2s[:], ps_m2[:]).then_inc(d_sev, 1)
                    CHV(v.tensor_mul(vars_[:], mus[:], mus[:]))
                    v.tensor_sub(vars_[:], m2s[:], vars_[:]).then_inc(d_sd, 1)
                    v.wait_ge(d_sa, 2 * i + hf + 1)
                    v.tensor_mul(mus[:], mus[:], rstd[:]).then_inc(d_sd2, 1)
                v.wait_ge(d_bc, 160 * i + 128)
                for ct in range(CT):
                    CHV(v.tensor_mul(sq[0][:], xin[:, ct * Q: ct * Q + H], rb[:, 0:H]))
                    CHV(v.tensor_mul(sq[1][:], xin[:, ct * Q + H: (ct + 1) * Q], rb[:, H:Q]))
                    CHV(v.tensor_sub(g[:, ct * Q: ct * Q + H], sq[0][:], mub[:, 0:H]))
                    v.tensor_sub(g[:, ct * Q + H: (ct + 1) * Q], sq[1][:], mub[:, H:Q]
                                 ).then_inc(d_ap, 1)
                for hf in range(2):
                    for ct in range(CT):
                        idx = 16 * i + 8 * hf + ct
                        if idx >= 2:
                            v.wait_ge(d_mmg, idx - 1)
                        v.tensor_copy(gbt[ct % 2][:], g[:, ct * Q + hf * H: ct * Q + (hf + 1) * H]
                                      ).then_inc(d_gb, 1)
                    v.wait_ge(d_mmx, 2 * i + hf + 1)
                    if hf == 0 and i > 0:
                        v.wait_ge(d_bc, 160 * i)
                    CHV(v.tensor_copy(Bsb[:, tsl(hf, H)], ps_B[:]))
                    v.tensor_copy(Csb[:, tsl(hf, H)], ps_C[:]).then_inc(d_evx, 1)
                v.wait_ge(d_evd, 16 * (i + 1))
                for dt in range(CT):
                    v.tensor_scalar_mul(xin[:, tsl(dt, Q)], g[:, tsl(dt, Q)],
                                        cv[:, CV_DP + dt:CV_DP + dt + 1])
                for ct in range(CT):
                    v.tensor_mul(g[:, tsl(ct, Q)], delta[:, tsl(ct, Q)], g[:, tsl(ct, Q)]
                                 ).then_inc(d_dg, 1)
                for n in range(16):
                    v.wait_ge(d_bcn, 512 * i + 32 * (n + 1))
                    for dt in range(CT):
                        idx = 128 * i + 8 * n + dt
                        v.wait_ge(d_aex, idx + 1)
                        par = idx % 2
                        v.tensor_mul(ut[par][:], g[:, tsl(dt, Q)], Brep[n % 2][:])
                        col = hstate[:, dt * 16 + n: dt * 16 + n + 1]
                        CHV(v.tensor_tensor_scan(ht[par][:], at[par][:], ut[par][:],
                                                 col, ALU.mult, ALU.add))
                        CHV(v.tensor_tensor(col, ht[par][:, Q - 1:Q],
                                            ht[par][:, Q - 1:Q], ALU.max))
                        if n == 15:
                            v.tensor_mul(yt[:], ht[par][:], Crep[n % 2][:]
                                         ).then_inc(d_scn, 1)
                            v.tensor_add(xin[:, tsl(dt, Q)], xin[:, tsl(dt, Q)],
                                         yt[:]).then_inc(d_fin, 1)
                        else:
                            v.tensor_mul(yt[:], ht[par][:], Crep[n % 2][:])
                            v.tensor_add(xin[:, tsl(dt, Q)], xin[:, tsl(dt, Q)],
                                         yt[:]).then_inc(d_scn, 1)
    return es
def _r8(v):
    return np.ascontiguousarray(np.asarray(v, np.float32).reshape(8, 128).T)

def _r32(v):
    return np.ascontiguousarray(np.asarray(v, np.float32).reshape(32, 128).T)

def _pack_cvec(d):
    cvp = np.zeros((128, NCV), np.float32)
    cvp[:, 0:8] = _r8(d["ts"].reshape(-1))
    cvp[:, 8:16] = _r8(d["u"])
    cvp[:, 16:24] = _r8(d["ln1w"])
    cvp[:, 24:32] = _r8(d["ln1b"])
    cvp[:, 32:40] = _r8(d["ln2w"])
    cvp[:, 40:48] = _r8(d["ln2b"])
    cvp[:, 48:56] = _r8(d["bdt"])
    cvp[:, 56:64] = _r8(d["Dp"])
    cvp[:, 64:72] = _r8(d["b2"])
    cvp[:, 72:104] = _r32(d["b1"])
    cvp[:, 104:136] = _r32(d["th"])
    cvp[:, 136:168] = _r32(d["qwr"])
    cvp[:, 168:200] = _r32(d["qwi"])
    al = np.asarray(d["alog"], np.float32)
    cvp[:, 200:328] = al.reshape(8, 128, 16).transpose(1, 0, 2).reshape(128, 128)
    return cvp

def _tr(a):
    return np.ascontiguousarray(np.asarray(a, np.float32).T)

def _build():
    nc = bass.Bass("TRN2", target_bir_lowering=False, debug=False)
    F32_ = F32
    ins = {}
    def I(name, shape):
        ins[name] = nc.dram_tensor(name, shape, F32_, kind="ExternalInput").ap()
        return ins[name]
    xT = I("xT", [C, T])
    cvec = I("cvec", [P, NCV])
    wkT = I("wkT", [1024, 1024]); wvT = I("wvT", [1024, 1024]); wrT = I("wrT", [1024, 1024])
    wckT = I("wckT", [1024, 4096]); wcvT = I("wcvT", [4096, 1024]); wcrT = I("wcrT", [1024, 1024])
    w1n = I("w1n", [1024, 4096]); w2n = I("w2n", [4096, 1024])
    wxpT = I("wxpT", [1024, 96]); wdtT = I("wdtT", [64, 1024])
    wfcT = I("wfcT", [1024, 4096]); wprojT = I("wprojT", [4096, 1024])
    outT = nc.dram_tensor("outT", [C, T], F32_, kind="ExternalOutput").ap()
    a_d = nc.dram_tensor("a_d", [C, T], F32_, kind="Internal").ap()
    x1_d = nc.dram_tensor("x1_d", [C, T], F32_, kind="Internal").ap()
    x2_d = nc.dram_tensor("x2_d", [C, T], F32_, kind="Internal").ap()
    dp_d = nc.dram_tensor("dp_d", [C, T], F32_, kind="Internal").ap()
    bcd = nc.dram_tensor("bcd", [8, 1024], F32_, kind="Internal").ap()
    bcdD = nc.dram_tensor("bcdD", [110, 1024], F32_, kind="Internal").ap()

    kb = KB(nc)
    kb.alloc_consts()
    kb.emit_consts(cvec)
    kb._ces.close()
    emit_phase_A(kb, xT, a_d, wkT, wvT, wrT, bcd).close()
    emit_phase_B(kb, xT, a_d, x1_d, wckT, wcvT, wcrT, bcd).close()
    emit_phase_C(kb, x1_d, x2_d, w1n, w2n, bcd).close()
    emit_phase_D(kb, x2_d, dp_d, wxpT, wdtT, bcdD).close()
    emit_phase_E(kb, x2_d, dp_d, outT, wfcT, wprojT, bcd).close()
    return nc


def kernel(L0_attn_x, L0_attn_time_decay, L0_attn_time_first, L0_attn_time_shift,
           L0_attn_Wk, L0_attn_Wv, L0_attn_Wr, L0_attn_Wck, L0_attn_Wcv, L0_attn_Wcr,
           L0_attn_ln1_w, L0_attn_ln1_b, L0_attn_ln2_w, L0_attn_ln2_b,
           L0_ffn_W1, L0_ffn_b1, L0_ffn_quantum_phases, L0_ffn_quantum_weights_real,
           L0_ffn_quantum_weights_imag, L0_ffn_W2, L0_ffn_b2,
           L1_attn_W_xproj, L1_attn_W_dt, L1_attn_b_dt, L1_attn_A_log, L1_attn_D_param,
           L1_ffn_W_fc, L1_ffn_W_proj):
    cvp = _pack_cvec({
        "ts": np.asarray(L0_attn_time_shift), "u": L0_attn_time_first,
        "ln1w": L0_attn_ln1_w, "ln1b": L0_attn_ln1_b,
        "ln2w": L0_attn_ln2_w, "ln2b": L0_attn_ln2_b,
        "bdt": L1_attn_b_dt, "Dp": L1_attn_D_param, "b2": L0_ffn_b2,
        "b1": L0_ffn_b1, "th": L0_ffn_quantum_phases,
        "qwr": L0_ffn_quantum_weights_real, "qwi": L0_ffn_quantum_weights_imag,
        "alog": L1_attn_A_log,
    })
    shared = {
        "cvec": cvp,
        "wkT": _tr(L0_attn_Wk), "wvT": _tr(L0_attn_Wv), "wrT": _tr(L0_attn_Wr),
        "wckT": _tr(L0_attn_Wck), "wcvT": _tr(L0_attn_Wcv), "wcrT": _tr(L0_attn_Wcr),
        "w1n": np.ascontiguousarray(np.asarray(L0_ffn_W1, np.float32)),
        "w2n": np.ascontiguousarray(np.asarray(L0_ffn_W2, np.float32)),
        "wxpT": _tr(L1_attn_W_xproj), "wdtT": _tr(L1_attn_W_dt),
        "wfcT": _tr(L1_ffn_W_fc), "wprojT": _tr(L1_ffn_W_proj),
    }
    x = np.asarray(L0_attn_x, np.float32)
    nc = _build()
    in_maps = [dict(shared, xT=_tr(x[b])) for b in range(8)]
    res = run_bass_kernel_spmd(nc, in_maps, core_ids=list(range(8)))
    out = np.stack([np.ascontiguousarray(res.results[b]["outT"].T) for b in range(8)], axis=0)
    return out.astype(np.float32)
